# version 15
# baseline (speedup 1.0000x reference)
import numpy as np
import ml_dtypes
from contextlib import ExitStack

import concourse.bass as bass
import concourse.mybir as mybir
from concourse.bass_utils import run_bass_kernel_spmd

F32 = mybir.dt.float32
BF16 = mybir.dt.bfloat16
AF = mybir.ActivationFunctionType
ALU = mybir.AluOpType
AX = mybir.AxisListType

NCORES = 8
E = 1024
I = 2816
TOK = 4096
TSUP = 512
NSUP = TOK // TSUP
NT = TSUP // 128
KCH = E // 128
ICH = I // 128
NSL = 256
NSLC = I // NSL

T2 = [0.0625, 0.5625, 1.5625, 3.0625, 6.25, 12.25, 25.0]
TW = [1.0, 1.0, 1.0, 1.0, 2.0, 2.0, 4.0]

_BF = ml_dtypes.bfloat16
_CONSTS = None
DEBUG = False


def _host_consts():
    global _CONSTS
    if _CONSTS is not None:
        return _CONSTS
    k = np.arange(1024, dtype=np.float64)
    th = np.outer(k, k) * (2.0 * np.pi / 1024.0)
    cos = np.cos(th)
    sin = np.sin(th)
    cos_hi = cos.astype(np.float32)
    cos_lo = (cos - cos_hi).astype(np.float32)
    sin_hi = sin.astype(np.float32)
    sin_lo = (sin - sin_hi).astype(np.float32)
    fr = np.fft.fftfreq(1024).astype(np.float32)
    kraw = fr.reshape(8, 128).T.copy()
    ident = np.eye(128, dtype=np.float32)
    identb = ident.astype(_BF)
    J = ident[::-1].copy()
    _A = np.array([[1,1,1,0,1,1,1,0,0,0,1,1],[1,1,0,1,1,1,0,0,0,1,1,0],
                   [1,0,1,1,1,0,0,0,1,1,0,1],[0,1,1,1,0,0,0,1,1,0,1,1],
                   [0,0,0,0,1,1,1,1,1,1,1,1],[1,1,1,1,0,0,0,0,0,0,0,0],
                   [0,0,0,0,1,1,1,1,0,0,0,0],[1,1,0,0,1,1,0,0,1,1,0,0],
                   [1,0,1,0,1,0,1,0,1,0,1,0],[0,1,0,1,0,1,0,1,0,1,0,1],
                   [1,0,0,1,1,0,0,1,1,0,0,1],[0,1,1,0,0,1,1,0,0,1,1,0]],
                  dtype=np.float32)
    Hm = np.concatenate([_A.T, np.eye(12, dtype=np.float32)], axis=1)
    Ht = Hm.T.astype(_BF)
    sc5 = np.array([[2.0, 4.0, 8.0, 16.0, 32.0]], dtype=np.float32)
    ones1 = np.ones((1, 128), dtype=np.float32)
    import jax
    import jax.numpy as jnp
    with jax.default_device(jax.devices("cpu")[0]):
        kj = jnp.asarray(fr)
        kaj = jnp.abs(kj) + 1e-08
        Dj = jnp.float32(1.0)
        omj = 1.0 + 0.2 * Dj
        aj = 1.0 + 0.5 * Dj
        bj = 0.1 * Dj
        realj = jnp.sin(omj * Dj + aj * kaj)
        phasej = omj * Dj - kaj + bj * kaj ** 2
        filtj = np.asarray(realj * jnp.exp(1j * phasej))
    frD1 = np.ascontiguousarray(filtj.real.astype(np.float32).reshape(8, 128).T)
    fiD1 = np.ascontiguousarray(filtj.imag.astype(np.float32).reshape(8, 128).T)
    _CONSTS = dict(cos_hi=cos_hi, cos_lo=cos_lo, sin_hi=sin_hi, sin_lo=sin_lo,
                   kraw=kraw, ident=ident, identb=identb, J=J, Ht=Ht, sc5=sc5,
                   ones1=ones1, frD1=frD1, fiD1=fiD1)
    return _CONSTS


def build_program():
    nc = bass.Bass("TRN2", target_bir_lowering=False, debug=False)

    dx = nc.dram_tensor("x", [TOK, E], F32, kind="ExternalInput").ap()
    dx0 = nc.dram_tensor("x0", [1, E], F32, kind="ExternalInput").ap()
    dwel = nc.dram_tensor("w_el", [E, 24], F32, kind="ExternalInput").ap()
    dbel = nc.dram_tensor("b_el", [1, 24], F32, kind="ExternalInput").ap()
    dwle = nc.dram_tensor("w_le", [24, E], F32, kind="ExternalInput").ap()
    dble = nc.dram_tensor("b_le", [128, 8], F32, kind="ExternalInput").ap()
    dwg = nc.dram_tensor("w_gate", [I, E], F32, kind="ExternalInput").ap()
    dwu = nc.dram_tensor("w_up", [I, E], F32, kind="ExternalInput").ap()
    dwd = nc.dram_tensor("w_down", [E, I], F32, kind="ExternalInput").ap()
    dcos = [nc.dram_tensor(n, [E, E], F32, kind="ExternalInput").ap()
            for n in ("cos_hi", "cos_lo", "sin_hi", "sin_lo")]
    dkraw = nc.dram_tensor("kraw", [128, 8], F32, kind="ExternalInput").ap()
    dfrD1 = nc.dram_tensor("frD1", [128, 8], F32, kind="ExternalInput").ap()
    dfiD1 = nc.dram_tensor("fiD1", [128, 8], F32, kind="ExternalInput").ap()
    dident = nc.dram_tensor("ident", [128, 128], F32, kind="ExternalInput").ap()
    didentb = nc.dram_tensor("identb", [128, 128], BF16, kind="ExternalInput").ap()
    dJ = nc.dram_tensor("J", [128, 128], F32, kind="ExternalInput").ap()
    dHt = nc.dram_tensor("Ht", [24, 12], BF16, kind="ExternalInput").ap()
    dsc5 = nc.dram_tensor("sc5", [1, 5], F32, kind="ExternalInput").ap()
    dones = nc.dram_tensor("ones1", [1, 128], F32, kind="ExternalInput").ap()
    dy = nc.dram_tensor("y", [TOK, E], F32, kind="ExternalOutput").ap()
    dc2 = nc.dram_tensor("c2buf", [1, 2048], F32, kind="Internal").ap()
    dChi = nc.dram_tensor("Chi", [E, E], BF16, kind="Internal").ap()
    dClo = nc.dram_tensor("Clo", [E, E], BF16, kind="Internal").ap()
    dwgt = [nc.dram_tensor("wgT_hi", [E, I], BF16, kind="Internal").ap(),
            nc.dram_tensor("wgT_lo", [E, I], BF16, kind="Internal").ap()]
    dwut = [nc.dram_tensor("wuT_hi", [E, I], BF16, kind="Internal").ap(),
            nc.dram_tensor("wuT_lo", [E, I], BF16, kind="Internal").ap()]
    dwdt = [nc.dram_tensor("wdT_hi", [I, E], BF16, kind="Internal").ap(),
            nc.dram_tensor("wdT_lo", [I, E], BF16, kind="Internal").ap()]

    ctx = ExitStack()
    SB = lambda n, s, d=F32: ctx.enter_context(nc.sbuf_tensor(n, s, d))
    PS = lambda n, s, d=F32: ctx.enter_context(nc.psum_tensor(n, s, d))
    SEM = lambda n: ctx.enter_context(nc.semaphore(n))

    ENGS = ("sync", "scalar", "vector", "tensor", "gpsimd")
    Q = {e: [] for e in ENGS}
    sems = {}
    cnt = {}
    semnames = ["dS", "dA", "dG", "sP", "sV", "sS", "sG", "dCon", "dC2", "dXre"]
    for pfx in ("dCos", "dWn", "dX", "dCc", "dWs", "dWd", "dSh"):
        semnames += [pfx + "0", pfx + "1"]
    for s in semnames:
        sems[s] = SEM(s)
        cnt[s] = 0

    def emit(eng, fn):
        Q[eng].append(fn)

    def wait(eng, sem, val):
        if val is None or val <= 0:
            return
        h = sems[sem]
        emit(eng, lambda e, h=h, v=val: e.wait_ge(h, v))

    def waitt(eng, tok):
        if tok is not None:
            wait(eng, tok[0], tok[1])

    def op(eng, fn, inc=None):
        if inc is None:
            emit(eng, lambda e, f=fn: f(e))
            return None
        cnt[inc] += 1
        tok = cnt[inc]
        h = sems[inc]
        emit(eng, lambda e, f=fn, h=h: f(e).then_inc(h, 1))
        return tok

    def dr(eng):
        emit(eng, lambda e: e.drain())

    def dma(eng, sem, out_ap, in_ap):
        cnt[sem] += 16
        tok = cnt[sem]
        h = sems[sem]
        emit(eng, lambda e, o=out_ap, i=in_ap, h=h: e.dma_start(out=o, in_=i).then_inc(h, 16))
        return tok

    bankfree = {}

    def bank_wait(eng, bank):
        if bank in bankfree:
            waitt(eng, bankfree[bank])

    def bank_set(bank, sem, tok):
        bankfree[bank] = (sem, tok)

    tapped = set()

    def tap(name, ap, *pretoks):
        if not DEBUG or name in tapped:
            return
        tapped.add(name)
        d = nc.dram_tensor("tap_" + name, list(ap.shape), ap.dtype,
                           kind="ExternalOutput").ap()
        for t in pretoks:
            waitt("gpsimd", t)
        dma("gpsimd", "dG", d, ap)

    mm_a = PS("mm_a", [128, 512], F32)
    mm_b = PS("mm_b", [128, 512], F32)
    mm_c = PS("mm_c", [128, 512], F32)
    mm_e = PS("mm_e", [128, 512], F32)
    tr_a = PS("tr_a", [128, 128], F32)
    tr_b = PS("tr_b", [128, 128], F32)
    tr_c = PS("tr_c", [128, 128], BF16)
    tr_d = PS("tr_d", [128, 128], BF16)
    MMB = {"mm_a": mm_a, "mm_b": mm_b, "mm_c": mm_c, "mm_e": mm_e}

    scrA = SB("scrA", [128, E])
    scrB = SB("scrB", [128, E])
    f_bf = SB("f_bf", [128, E], BF16)
    t_bf = SB("t_bf", [128, E], BF16)
    mbq = [SB("mbq%d" % i, [128, E], BF16) for i in range(3)]
    sgn_bf = SB("sgn_bf", [128, E], BF16)
    sp_hi = SB("sp_hi", [128, E], BF16)
    sp_lo = SB("sp_lo", [128, E], BF16)
    bm_s = SB("bm_s", [128, 176])
    sc_s = SB("sc_s", [128, 176])
    sh_s = SB("sh_s", [128, 176])
    rc_s = SB("rc_s", [128, 176])
    sum_s = SB("sum_s", [128, 1])
    ssq_s = SB("ssq_s", [128, 1])
    m_s = SB("m_s", [128, 1])
    m2_s = SB("m2_s", [128, 1])
    var_s = SB("var_s", [128, 1])
    sq_s = SB("sq_s", [128, 1])
    rstd_s = SB("rstd_s", [128, 1])
    wel_b = [SB("wel_hi", [128, KCH * 24], BF16), SB("wel_lo", [128, KCH * 24], BF16)]
    wle_b = [SB("wle_hi", [24, E], BF16), SB("wle_lo", [24, E], BF16)]
    negb_s = SB("negb_s", [128, 24])
    ble_s = SB("ble_s", [128, 8])
    bits_s = SB("bits_s", [128, 24])
    syn_s = SB("syn_s", [128, 12], BF16)
    wgol_s = SB("wgol_s", [128, 1])
    fixm_s = SB("fixm_s", [128, 1])
    c2_s = SB("c2_s", [128, 1])
    p12_s = SB("p12_s", [128, 12], BF16)
    r12_s = SB("r12_s", [128, 12], BF16)
    t12_s = SB("t12_s", [128, 12], BF16)
    corr_s = SB("corr_s", [128, 24])
    bitsT_s = SB("bitsT_s", [24, 128], BF16)
    corrT_s = SB("corrT_s", [24, 512], BF16)
    idn_s = SB("idn_s", [128, 128])
    idnb_s = SB("idnb_s", [128, 128], BF16)
    ht_s = SB("ht_s", [24, 12], BF16)
    kraw_s = SB("kraw_s", [128, 8])
    ka_s = SB("ka_s", [128, 8])
    ka2_s = SB("ka2_s", [128, 8])
    fr_s = SB("fr_s", [128, 8])
    fi_s = SB("fi_s", [128, 8])
    nfi_s = SB("nfi_s", [128, 8])
    real_s = SB("real_s", [128, 8])
    ph_s = SB("ph_s", [128, 8])
    sph_s = SB("sph_s", [128, 8])
    cph_s = SB("cph_s", [128, 8])
    sc1 = {n: SB("s1_" + n, [128, 1]) for n in
           ("six", "five", "ten", "one", "r6", "fifth", "tenth", "pi4", "pi", "pihalf",
            "Drep", "omega", "aco", "nega", "bco", "omD", "pmo", "sel")}
    frD1_s = SB("frD1_s", [128, 8])
    fiD1_s = SB("fiD1_s", [128, 8])
    ones_s = SB("ones_s", [1, 128])

    last_s_guard = [("sS", 0)]
    last_g_guard = [None]

    def quant(src, N, bufs=None, gq=False):
        if bufs is None:
            qb, bmq, scq, shq, rcq = scrB, bm_s, sc_s, sh_s, rc_s
            A, B, Cb, D, Eb, sgq = t_bf, f_bf, mbq[0], mbq[1], mbq[2], sgn_bf
        else:
            qb, bmq, scq, shq, rcq, A, B, Cb, D, Eb, sgq = bufs
        nb = N // 16
        waitt("vector", last_s_guard[0])
        if last_g_guard[0] is not None:
            waitt("vector", last_g_guard[0])
        srcv = src[:, 0:N].rearrange("p (b s) -> p b s", s=16)
        dr("vector")
        op("vector", lambda e: e.tensor_reduce(out=bmq[:, 0:nb], in_=srcv, axis=AX.X,
                                               op=ALU.max, apply_absolute_value=True))
        dr("vector")
        op("vector", lambda e: e.tensor_scalar(out=scq[:, 0:nb], in0=bmq[:, 0:nb],
                                               scalar1=sc1["r6"][:], scalar2=1e-12,
                                               op0=ALU.mult, op1=ALU.max))
        dr("vector")
        op("vector", lambda e: e.tensor_single_scalar(out=shq[:, 0:nb], in_=scq[:, 0:nb],
                                                      scalar=0.5, op=ALU.mult))
        rt = op("vector", lambda e: e.reciprocal(out=rcq[:, 0:nb], in_=scq[:, 0:nb]),
                inc="sV" if gq else None)
        if not gq:
            dr("vector")
        rcb = rcq[:, 0:nb].rearrange("p (b o) -> p b o", o=1)
        rcb2, srcv2 = bass.broadcast_tensor_aps(rcb, srcv)
        uv = qb[:, 0:N].rearrange("p (b s) -> p b s", s=16)
        ueng = "gpsimd" if gq else "vector"
        if gq:
            wait("gpsimd", "sV", rt)
        vt = op(ueng, lambda e: e.tensor_tensor(out=uv, in0=srcv2, in1=rcb2,
                                                op=ALU.mult), inc="sG" if gq else "sV")
        wait("scalar", "sG" if gq else "sV", vt)
        op("scalar", lambda e: e.activation(out=qb[:, 0:N], in_=qb[:, 0:N],
                                            func=AF.Square))
        st = op("scalar", lambda e: e.sign(out=sgq[:, 0:N], in_=src[:, 0:N]), inc="sS")
        wait("vector", "sS", st)
        def cmpi(e, dst, i):
            if TW[i] == 1.0:
                return e.tensor_single_scalar(out=dst[:, 0:N], in_=qb[:, 0:N],
                                              scalar=T2[i], op=ALU.is_gt)
            return e.tensor_scalar(out=dst[:, 0:N], in0=qb[:, 0:N], scalar1=T2[i],
                                   scalar2=TW[i], op0=ALU.is_gt, op1=ALU.mult)
        if gq:
            op("vector", lambda e: cmpi(e, A, 0))
            op("vector", lambda e: cmpi(e, B, 1))
            op("vector", lambda e: cmpi(e, Cb, 2))
            op("vector", lambda e: cmpi(e, D, 3))
            v1 = op("vector", lambda e: cmpi(e, Eb, 4), inc="sV")
            wait("gpsimd", "sV", v1)
            op("gpsimd", lambda e: e.tensor_tensor(out=A[:, 0:N], in0=A[:, 0:N],
                                                   in1=B[:, 0:N], op=ALU.add))
            g1 = op("gpsimd", lambda e: e.tensor_tensor(out=Cb[:, 0:N], in0=Cb[:, 0:N],
                                                        in1=D[:, 0:N], op=ALU.add),
                    inc="sG")
            wait("vector", "sG", g1)
            op("vector", lambda e: cmpi(e, B, 5))
            v2 = op("vector", lambda e: cmpi(e, D, 6), inc="sV")
            wait("gpsimd", "sV", v2)
            op("gpsimd", lambda e: e.tensor_tensor(out=Eb[:, 0:N], in0=Eb[:, 0:N],
                                                   in1=B[:, 0:N], op=ALU.add))
            op("gpsimd", lambda e: e.tensor_tensor(out=A[:, 0:N], in0=A[:, 0:N],
                                                   in1=Cb[:, 0:N], op=ALU.add))
            op("gpsimd", lambda e: e.tensor_tensor(out=A[:, 0:N], in0=A[:, 0:N],
                                                   in1=Eb[:, 0:N], op=ALU.add))
            g2 = op("gpsimd", lambda e: e.tensor_tensor(out=A[:, 0:N], in0=A[:, 0:N],
                                                        in1=D[:, 0:N], op=ALU.add),
                    inc="sG")
            wait("vector", "sG", g2)
            v3 = op("vector", lambda e: e.tensor_tensor(out=B[:, 0:N], in0=A[:, 0:N],
                                                        in1=sgq[:, 0:N], op=ALU.mult),
                    inc="sV")
            wait("gpsimd", "sV", v3)
            shb = shq[:, 0:nb].rearrange("p (b o) -> p b o", o=1)
            fv = B[:, 0:N].rearrange("p (b s) -> p b s", s=16)
            shb2, fv2 = bass.broadcast_tensor_aps(shb, fv)
            qv = qb[:, 0:N].rearrange("p (b s) -> p b s", s=16)
            vt = op("gpsimd", lambda e: e.tensor_tensor(out=qv, in0=fv2, in1=shb2,
                                                        op=ALU.mult), inc="sG")
            last_s_guard[0] = ("sS", st)
            last_g_guard[0] = ("sG", vt)
            return ("sG", vt)
        op("vector", lambda e: cmpi(e, A, 0))
        op("vector", lambda e: cmpi(e, B, 1))
        op("vector", lambda e: cmpi(e, Cb, 2))
        op("vector", lambda e: cmpi(e, D, 3))
        op("vector", lambda e: cmpi(e, Eb, 4))
        dr("vector")
        op("vector", lambda e: e.tensor_tensor(out=A[:, 0:N], in0=A[:, 0:N],
                                               in1=B[:, 0:N], op=ALU.add))
        op("vector", lambda e: e.tensor_tensor(out=Cb[:, 0:N], in0=Cb[:, 0:N],
                                               in1=D[:, 0:N], op=ALU.add))
        dr("vector")
        op("vector", lambda e: cmpi(e, B, 5))
        op("vector", lambda e: cmpi(e, D, 6))
        op("vector", lambda e: e.tensor_tensor(out=A[:, 0:N], in0=A[:, 0:N],
                                               in1=Cb[:, 0:N], op=ALU.add))
        dr("vector")
        op("vector", lambda e: e.tensor_tensor(out=B[:, 0:N], in0=B[:, 0:N],
                                               in1=D[:, 0:N], op=ALU.add))
        dr("vector")
        op("vector", lambda e: e.tensor_tensor(out=A[:, 0:N], in0=A[:, 0:N],
                                               in1=B[:, 0:N], op=ALU.add))
        dr("vector")
        op("vector", lambda e: e.tensor_tensor(out=A[:, 0:N], in0=A[:, 0:N],
                                               in1=Eb[:, 0:N], op=ALU.add))
        dr("vector")
        op("vector", lambda e: e.tensor_tensor(out=B[:, 0:N], in0=A[:, 0:N],
                                               in1=sgq[:, 0:N], op=ALU.mult))
        dr("vector")
        shb = shq[:, 0:nb].rearrange("p (b o) -> p b o", o=1)
        fv = B[:, 0:N].rearrange("p (b s) -> p b s", s=16)
        shb2, fv2 = bass.broadcast_tensor_aps(shb, fv)
        qv = qb[:, 0:N].rearrange("p (b s) -> p b s", s=16)
        vt = op("vector", lambda e: e.tensor_tensor(out=qv, in0=fv2, in1=shb2,
                                                    op=ALU.mult), inc="sV")
        last_s_guard[0] = ("sS", st)
        return ("sV", vt)

    sp_guard = {}

    def split(src, N, hi_dst, lo_dst, pre, gname):
        waitt("scalar", pre)
        if gname in sp_guard:
            wait("scalar", "sP", sp_guard[gname])
        st = op("scalar", lambda e: e.copy(out=hi_dst[:, 0:N], in_=src[:, 0:N]), inc="sS")
        wait("vector", "sS", st)
        vt = op("vector", lambda e: e.scalar_tensor_tensor(
            out=lo_dst[:, 0:N], in0=hi_dst[:, 0:N], scalar=-1.0, in1=src[:, 0:N],
            op0=ALU.mult, op1=ALU.add), inc="sV")
        last_s_guard[0] = ("sS", st)
        return ("sS", st), ("sV", vt)

    trrot = [0]

    def transpose_bf16(src_ap, dst_ap, pre_v):
        nmi = trrot[0] % 2
        bank = (tr_c, tr_d)[nmi]
        bname = "trcd%d" % nmi
        trrot[0] += 1
        bank_wait("tensor", bname)
        waitt("tensor", pre_v)
        pt = op("tensor", lambda e, b=bank, s=src_ap: e.matmul(
            out=b[:], lhsT=s, rhs=idnb_s[:], is_transpose=True, start=True, stop=True),
            inc="sP")
        wait("scalar", "sP", pt)
        stok = op("scalar", lambda e, b=bank, d=dst_ap: e.copy(out=d, in_=b[:]), inc="sS")
        bank_set(bname, "sS", stok)
        return ("sS", stok), pt

    f32rot = [0]

    def transpose_f32(src_ap, pre_tok, rows=128):
        nmi = f32rot[0] % 2
        bank = (tr_a, tr_b)[nmi]
        bname = "trab%d" % nmi
        f32rot[0] += 1
        bank_wait("tensor", bname)
        waitt("tensor", pre_tok)
        pt = op("tensor", lambda e, b=bank, s=src_ap, r=rows: e.matmul(
            out=b[0:r, :], lhsT=s, rhs=idn_s[:], is_transpose=True, start=True, stop=True),
            inc="sP")
        return bank, pt, bname

    ictx = ExitStack()
    ISB = lambda n, s, d=F32: ictx.enter_context(nc.sbuf_tensor(n, s, d))
    bel_s = ISB("bel_s", [1, 24])
    wle_f = ISB("wle_f", [24, E])
    wel_f = ISB("wel_f", [128, KCH * 24])
    fctx = ExitStack()
    FSB = lambda n, s, d=F32: fctx.enter_context(nc.sbuf_tensor(n, s, d))
    x0_s = FSB("x0_s", [1, E])
    xn0_s = FSB("xn0_s", [1, E])
    cnts_s = FSB("cnts_s", [1, 8])
    lx_s = FSB("lx_s", [1, 5])
    ly_s = FSB("ly_s", [1, 5])
    red_s = FSB("red_s", [1, 8])
    tmp5_s = FSB("tmp5_s", [1, 5])
    sc5_s = FSB("sc5_s", [1, 5])
    J_s = FSB("J_s", [128, 128])
    cosbuf = [[FSB("cosb_%d_%d" % (i, b2), [128, E]) for b2 in range(2)] for i in range(4)]
    c_sb = FSB("c_sb", [1, E])
    sh_buf = [FSB("sh_buf0", [128, E]), FSB("sh_buf1", [128, E])]
    cchi_s = FSB("cchi_s", [128, E], BF16)
    cclo_s = FSB("cclo_s", [128, E], BF16)

    t_id = dma("sync", "dCon", idn_s[:], dident)
    t_idb = dma("sync", "dCon", idnb_s[:], didentb)
    t_ht = dma("sync", "dCon", ht_s[:], dHt)
    t_kraw = dma("sync", "dCon", kraw_s[:], dkraw)
    t_sc5 = dma("sync", "dCon", sc5_s[:], dsc5)
    t_ones = dma("sync", "dCon", ones_s[:], dones)
    t_x0 = dma("sync", "dCon", x0_s[:], dx0)
    t_J = dma("sync", "dCon", J_s[:], dJ)
    t_wle = dma("sync", "dCon", wle_f[:], dwle)
    wel_src = bass.AP(tensor=dwel.tensor, offset=0, ap=[[24, 128], [128 * 24, KCH], [1, 24]])
    t_wel = dma("sync", "dCon", wel_f[:].rearrange("p (c j) -> p c j", j=24), wel_src)
    t_ble = dma("sync", "dCon", ble_s[:], dble)
    t_bel = dma("sync", "dCon", bel_s[:], dbel)
    dma("sync", "dCon", frD1_s[:], dfrD1)
    dma("sync", "dCon", fiD1_s[:], dfiD1)
    CONT = cnt["dCon"]
    t_id = t_idb = t_ht = t_kraw = t_sc5 = t_ones = t_x0 = t_J = CONT
    t_wle = t_wel = t_ble = t_bel = CONT

    wait("vector", "dCon", t_ones)
    for n, v in (("six", 6.0), ("five", 5.0), ("ten", 10.0), ("one", 1.0)):
        op("vector", lambda e, n=n, v=v: e.memset(sc1[n][:], v))
    dr("vector")
    op("vector", lambda e: e.reciprocal(out=sc1["r6"][:], in_=sc1["six"][:]))
    op("vector", lambda e: e.reciprocal(out=sc1["fifth"][:], in_=sc1["five"][:]))
    vt = op("vector", lambda e: e.reciprocal(out=sc1["tenth"][:], in_=sc1["ten"][:]), inc="sV")
    wait("scalar", "sV", vt)
    st = op("scalar", lambda e: e.activation(out=sc1["pi4"][:], in_=sc1["one"][:],
                                             func=AF.Arctan), inc="sS")
    wait("vector", "sS", st)
    op("vector", lambda e: e.tensor_single_scalar(out=sc1["pi"][:], in_=sc1["pi4"][:],
                                                  scalar=4.0, op=ALU.mult))
    dr("vector")
    op("vector", lambda e: e.tensor_single_scalar(out=sc1["pihalf"][:], in_=sc1["pi"][:],
                                                  scalar=0.5, op=ALU.mult))

    wait("scalar", "dCon", t_x0)
    st = op("scalar", lambda e: e.activation(out=scrB[0:1, :], in_=x0_s[:], func=AF.Square,
                                             accum_out=ssq_s[0:1, :]), inc="sS")
    wait("vector", "sS", st)
    wait("vector", "dCon", t_x0)
    op("vector", lambda e: e.reduce_sum(out=sum_s[0:1, :], in_=x0_s[:], axis=AX.X))
    dr("vector")
    op("vector", lambda e: e.tensor_single_scalar(out=m_s[0:1, :], in_=sum_s[0:1, :],
                                                  scalar=1.0 / 1024, op=ALU.mult))
    dr("vector")
    op("vector", lambda e: e.tensor_tensor(out=m2_s[0:1, :], in0=m_s[0:1, :],
                                           in1=m_s[0:1, :], op=ALU.mult))
    dr("vector")
    op("vector", lambda e: e.scalar_tensor_tensor(out=var_s[0:1, :], in0=ssq_s[0:1, :],
                                                  scalar=1.0 / 1024, in1=m2_s[0:1, :],
                                                  op0=ALU.mult, op1=ALU.subtract))
    dr("vector")
    vt = op("vector", lambda e: e.tensor_single_scalar(out=var_s[0:1, :], in_=var_s[0:1, :],
                                                       scalar=1e-5, op=ALU.add), inc="sV")
    wait("scalar", "sV", vt)
    st = op("scalar", lambda e: e.activation(out=sq_s[0:1, :], in_=var_s[0:1, :],
                                             func=AF.Sqrt), inc="sS")
    wait("vector", "sS", st)
    op("vector", lambda e: e.reciprocal(out=rstd_s[0:1, :], in_=sq_s[0:1, :]))
    dr("vector")
    vtdbg = op("vector", lambda e: e.tensor_scalar(out=xn0_s[:], in0=x0_s[:], scalar1=m_s[0:1, :],
                                           scalar2=rstd_s[0:1, :], op0=ALU.subtract,
                                           op1=ALU.mult), inc="sV")
    tap("x0row", x0_s[0:1, :], ("sV", vtdbg))
    tap("lnstats", ssq_s[0:1, :], ("sV", vtdbg))
    tap("lnsum", sum_s[0:1, :], ("sV", vtdbg))
    tap("lnvar", var_s[0:1, :], ("sV", vtdbg))
    tap("lnsq", sq_s[0:1, :], ("sV", vtdbg))
    tap("lnrstd", rstd_s[0:1, :], ("sV", vtdbg))
    tap("xn0_early", xn0_s[0:1, :], ("sV", vtdbg))
    for i, n2 in enumerate((4, 16, 64, 256, 1024)):
        dr("vector")
        op("vector", lambda e, n=n2: e.reduce_sum(out=m_s[0:1, :], in_=xn0_s[0:1, 0:n],
                                                  axis=AX.X))
        dr("vector")
        op("vector", lambda e, n=n2: e.tensor_single_scalar(out=m_s[0:1, :], in_=m_s[0:1, :],
                                                            scalar=1.0 / n, op=ALU.mult))
        dr("vector")
        op("vector", lambda e, n=n2: e.tensor_single_scalar(
            out=scrB[0:1, 0:n], in_=xn0_s[0:1, 0:n], scalar=m_s[0:1, :], op=ALU.is_gt))
        dr("vector")
        op("vector", lambda e, n=n2, i=i: e.reduce_sum(out=cnts_s[0:1, i:i + 1],
                                                       in_=scrB[0:1, 0:n], axis=AX.X))
        dr("vector")
        op("vector", lambda e, i=i: e.tensor_single_scalar(
            out=cnts_s[0:1, i:i + 1], in_=cnts_s[0:1, i:i + 1], scalar=1.0, op=ALU.max))
    dr("vector")
    vt = op("vector", lambda e: e.tensor_copy(out=red_s[0:1, 0:1], in_=cnts_s[0:1, 0:1]),
            inc="sV")
    wait("scalar", "sV", vt)
    wait("scalar", "dCon", t_sc5)
    op("scalar", lambda e: e.activation(out=lx_s[:], in_=sc5_s[:], func=AF.Ln))
    st = op("scalar", lambda e: e.activation(out=ly_s[:], in_=cnts_s[0:1, 0:5],
                                             func=AF.Ln), inc="sS")
    wait("vector", "sS", st)
    op("vector", lambda e: e.reduce_sum(out=red_s[0:1, 0:1], in_=lx_s[:], axis=AX.X))
    op("vector", lambda e: e.reduce_sum(out=red_s[0:1, 1:2], in_=ly_s[:], axis=AX.X))
    op("vector", lambda e: e.tensor_tensor(out=tmp5_s[:], in0=lx_s[:], in1=ly_s[:],
                                           op=ALU.mult))
    dr("vector")
    op("vector", lambda e: e.reduce_sum(out=red_s[0:1, 2:3], in_=tmp5_s[:], axis=AX.X))
    dr("vector")
    op("vector", lambda e: e.tensor_tensor(out=tmp5_s[:], in0=lx_s[:], in1=lx_s[:],
                                           op=ALU.mult))
    dr("vector")
    op("vector", lambda e: e.reduce_sum(out=red_s[0:1, 3:4], in_=tmp5_s[:], axis=AX.X))
    dr("vector")
    op("vector", lambda e: e.tensor_tensor(out=red_s[0:1, 4:5], in0=red_s[0:1, 0:1],
                                           in1=red_s[0:1, 1:2], op=ALU.mult))
    dr("vector")
    op("vector", lambda e: e.scalar_tensor_tensor(out=red_s[0:1, 4:5], in0=red_s[0:1, 2:3],
                                                  scalar=5.0, in1=red_s[0:1, 4:5],
                                                  op0=ALU.mult, op1=ALU.subtract))
    op("vector", lambda e: e.tensor_tensor(out=red_s[0:1, 5:6], in0=red_s[0:1, 0:1],
                                           in1=red_s[0:1, 0:1], op=ALU.mult))
    dr("vector")
    op("vector", lambda e: e.scalar_tensor_tensor(out=red_s[0:1, 5:6], in0=red_s[0:1, 3:4],
                                                  scalar=5.0, in1=red_s[0:1, 5:6],
                                                  op0=ALU.mult, op1=ALU.subtract))
    dr("vector")
    op("vector", lambda e: e.reciprocal(out=red_s[0:1, 6:7], in_=red_s[0:1, 5:6]))
    dr("vector")
    op("vector", lambda e: e.tensor_tensor(out=red_s[0:1, 7:8], in0=red_s[0:1, 4:5],
                                           in1=red_s[0:1, 6:7], op=ALU.mult))
    dr("vector")
    op("vector", lambda e: e.tensor_scalar(out=red_s[0:1, 7:8], in0=red_s[0:1, 7:8],
                                           scalar1=-1.0, scalar2=1.0, op0=ALU.mult,
                                           op1=ALU.max))
    dr("vector")
    vt = op("vector", lambda e: e.tensor_single_scalar(out=red_s[0:1, 7:8],
                                                       in_=red_s[0:1, 7:8], scalar=2.0,
                                                       op=ALU.min), inc="sV")
    tap("D", red_s[0:1, 0:8], ("sV", vt))
    tap("cnts", cnts_s[0:1, 0:5], ("sV", vt))
    tap("xn0", xn0_s[0:1, :], ("sV", vt))
    wait("tensor", "sV", vt)
    wait("tensor", "dCon", t_ones)
    pt = op("tensor", lambda e: e.matmul(out=mm_c[:, 0:1], lhsT=ones_s[:],
                                         rhs=red_s[0:1, 7:8], start=True, stop=True),
            inc="sP")
    wait("vector", "sP", pt)
    op("vector", lambda e: e.tensor_copy(out=sc1["Drep"][:], in_=mm_c[:, 0:1]))
    dr("vector")
    op("vector", lambda e: e.tensor_scalar(out=sc1["omega"][:], in0=sc1["Drep"][:],
                                           scalar1=sc1["fifth"][:], scalar2=1.0,
                                           op0=ALU.mult, op1=ALU.add))
    op("vector", lambda e: e.tensor_scalar(out=sc1["aco"][:], in0=sc1["Drep"][:],
                                           scalar1=0.5, scalar2=1.0, op0=ALU.mult,
                                           op1=ALU.add))
    dr("vector")
    op("vector", lambda e: e.tensor_single_scalar(out=sc1["nega"][:], in_=sc1["aco"][:],
                                                  scalar=-1.0, op=ALU.mult))
    op("vector", lambda e: e.tensor_tensor(out=sc1["bco"][:], in0=sc1["Drep"][:],
                                           in1=sc1["tenth"][:], op=ALU.mult))
    op("vector", lambda e: e.tensor_tensor(out=sc1["omD"][:], in0=sc1["omega"][:],
                                           in1=sc1["Drep"][:], op=ALU.mult))
    dr("vector")
    op("vector", lambda e: e.tensor_tensor(out=sc1["pmo"][:], in0=sc1["pi"][:],
                                           in1=sc1["omD"][:], op=ALU.subtract))
    dr("vector")
    bank_set("mm_c", "sV", op("vector", lambda e: e.tensor_copy(
        out=red_s[0:1, 1:2], in_=sc1["pmo"][0:1, :]), inc="sV"))
    wait("scalar", "dCon", t_kraw)
    st = op("scalar", lambda e: e.activation(out=ka_s[:], in_=kraw_s[:], func=AF.Abs),
            inc="sS")
    wait("vector", "sS", st)
    op("vector", lambda e: e.tensor_single_scalar(out=ka_s[:], in_=ka_s[:],
                                                  scalar=1e-8, op=ALU.add))
    dr("vector")
    op("vector", lambda e: e.tensor_tensor(out=ka2_s[:], in0=ka_s[:], in1=ka_s[:],
                                           op=ALU.mult))
    dr("vector")
    vt = op("vector", lambda e: e.tensor_scalar(out=real_s[:], in0=ka_s[:],
                                                scalar1=sc1["nega"][:],
                                                scalar2=sc1["pmo"][:], op0=ALU.mult,
                                                op1=ALU.add), inc="sV")
    wait("scalar", "sV", vt)
    st_r = op("scalar", lambda e: e.activation(out=real_s[:], in_=real_s[:], func=AF.Sin),
              inc="sS")
    op("vector", lambda e: e.tensor_scalar(out=ph_s[:], in0=ka2_s[:], scalar1=sc1["bco"][:],
                                           scalar2=sc1["omD"][:], op0=ALU.mult, op1=ALU.add))
    dr("vector")
    vt = op("vector", lambda e: e.tensor_tensor(out=ph_s[:], in0=ph_s[:], in1=ka_s[:],
                                                op=ALU.subtract), inc="sV")
    wait("scalar", "sV", vt)
    op("scalar", lambda e: e.activation(out=sph_s[:], in_=ph_s[:], func=AF.Sin))
    wait("vector", "sS", st_r)
    dr("vector")
    vt = op("vector", lambda e: e.tensor_scalar(out=cph_s[:], in0=ph_s[:], scalar1=-1.0,
                                                scalar2=sc1["pihalf"][:], op0=ALU.mult,
                                                op1=ALU.add), inc="sV")
    wait("scalar", "sV", vt)
    st = op("scalar", lambda e: e.activation(out=cph_s[:], in_=cph_s[:], func=AF.Sin),
            inc="sS")
    wait("vector", "sS", st)
    op("vector", lambda e: e.tensor_tensor(out=fr_s[:], in0=real_s[:], in1=cph_s[:],
                                           op=ALU.mult))
    op("vector", lambda e: e.tensor_tensor(out=fi_s[:], in0=real_s[:], in1=sph_s[:],
                                           op=ALU.mult))
    dr("vector")
    op("vector", lambda e: e.tensor_single_scalar(out=sc1["sel"][:], in_=sc1["Drep"][:],
                                                  scalar=1.0, op=ALU.is_equal))
    dr("vector")
    op("vector", lambda e: e.tensor_tensor(out=real_s[:], in0=frD1_s[:], in1=fr_s[:],
                                           op=ALU.subtract))
    op("vector", lambda e: e.tensor_tensor(out=ph_s[:], in0=fiD1_s[:], in1=fi_s[:],
                                           op=ALU.subtract))
    dr("vector")
    op("vector", lambda e: e.scalar_tensor_tensor(out=fr_s[:], in0=real_s[:],
                                                  scalar=sc1["sel"][:], in1=fr_s[:],
                                                  op0=ALU.mult, op1=ALU.add))
    op("vector", lambda e: e.scalar_tensor_tensor(out=fi_s[:], in0=ph_s[:],
                                                  scalar=sc1["sel"][:], in1=fi_s[:],
                                                  op0=ALU.mult, op1=ALU.add))
    dr("vector")
    vt_fr = op("vector", lambda e: e.tensor_single_scalar(out=nfi_s[:], in_=fi_s[:],
                                                          scalar=-1.0, op=ALU.mult),
               inc="sV")
    tap("fr", fr_s[:], ("sV", vt_fr))
    tap("fi", fi_s[:], ("sV", vt_fr))

    cchunk_done = {}
    bank_wait("tensor", "mm_c")
    for c in range(KCH):
        b = c % 2
        toks = []
        for a in range(4):
            if c >= 2:
                wait("sync", "sP", cchunk_done[c - 2])
            toks.append(dma("sync", "dCos%d" % b, cosbuf[a][b][:],
                            dcos[a][c * 128:(c + 1) * 128, :]))
        wait("tensor", "dCos%d" % b, toks[3])
        if c == 0:
            wait("tensor", "sV", vt_fr)
        frc = fr_s[:, c:c + 1]
        nfc = nfi_s[:, c:c + 1]
        tk = None
        for half, bank in ((0, mm_c), (1, mm_e)):
            sl = slice(half * 512, half * 512 + 512)
            op("tensor", lambda e, b2=bank, f=frc, s=sl, b3=b, sf=(c == 0): e.matmul(
                out=b2[0:1, :], lhsT=f, rhs=cosbuf[0][b3][:, s], start=sf, stop=False))
            op("tensor", lambda e, b2=bank, f=frc, s=sl, b3=b: e.matmul(
                out=b2[0:1, :], lhsT=f, rhs=cosbuf[1][b3][:, s], start=False, stop=False))
            op("tensor", lambda e, b2=bank, f=nfc, s=sl, b3=b: e.matmul(
                out=b2[0:1, :], lhsT=f, rhs=cosbuf[2][b3][:, s], start=False, stop=False))
            tk = op("tensor", lambda e, b2=bank, f=nfc, s=sl, b3=b, sp=(c == KCH - 1):
                    e.matmul(out=b2[0:1, :], lhsT=f, rhs=cosbuf[3][b3][:, s], start=False,
                             stop=sp), inc="sP")
        cchunk_done[c] = tk
    wait("vector", "sP", cchunk_done[KCH - 1])
    op("vector", lambda e: e.tensor_single_scalar(out=c_sb[0:1, 0:512], in_=mm_c[0:1, :],
                                                  scalar=1.0 / 1024, op=ALU.mult))
    vt = op("vector", lambda e: e.tensor_single_scalar(out=c_sb[0:1, 512:1024],
                                                       in_=mm_e[0:1, :],
                                                       scalar=1.0 / 1024, op=ALU.mult),
            inc="sV")
    bank_set("mm_c", "sV", vt)
    bank_set("mm_e", "sV", vt)
    tap("cvec", c_sb[0:1, :], ("sV", vt))
    wait("scalar", "sV", vt)
    dma("scalar", "dC2", dc2[0:1, 0:1024], c_sb[:])
    tc2 = dma("scalar", "dC2", dc2[0:1, 1024:2048], c_sb[:])
    shdone = {}
    for c in range(KCH):
        M0 = c * 128
        b = c % 2
        wait("gpsimd", "dC2", tc2)
        if c >= 2:
            wait("gpsimd", "sP", shdone[c - 2])
        src = bass.AP(tensor=dc2.tensor, offset=1024 - 127 - M0, ap=[[1, 128], [1, 1024]])
        tsh = dma("gpsimd", "dSh%d" % b, sh_buf[b][:], src)
        wait("tensor", "dSh%d" % b, tsh)
        if c == 0:
            wait("tensor", "dCon", t_J)
        bank_wait("tensor", "mm_a")
        bank_wait("tensor", "mm_b")
        op("tensor", lambda e, b2=b: e.matmul(out=mm_a[:], lhsT=J_s[:],
                                              rhs=sh_buf[b2][:, 0:512], start=True,
                                              stop=True))
        tk = op("tensor", lambda e, b2=b: e.matmul(out=mm_b[:], lhsT=J_s[:],
                                                   rhs=sh_buf[b2][:, 512:1024],
                                                   start=True, stop=True), inc="sP")
        shdone[c] = tk
        wait("scalar", "sP", tk)
        if c >= 1:
            wait("scalar", "dA", cstore)
        op("scalar", lambda e: e.copy(out=cchi_s[:, 0:512], in_=mm_a[:]))
        st = op("scalar", lambda e: e.copy(out=cchi_s[:, 512:1024], in_=mm_b[:]), inc="sS")
        wait("vector", "sS", st)
        op("vector", lambda e: e.scalar_tensor_tensor(out=cclo_s[:, 0:512],
                                                      in0=cchi_s[:, 0:512], scalar=-1.0,
                                                      in1=mm_a[:], op0=ALU.mult,
                                                      op1=ALU.add))
        vt = op("vector", lambda e: e.scalar_tensor_tensor(out=cclo_s[:, 512:1024],
                                                           in0=cchi_s[:, 512:1024],
                                                           scalar=-1.0, in1=mm_b[:],
                                                           op0=ALU.mult, op1=ALU.add),
                inc="sV")
        bank_set("mm_a", "sV", vt)
        bank_set("mm_b", "sV", vt)
        if c == 0:
            tap("Chi0", cchi_s[:], ("sV", vt))
        wait("scalar", "sV", vt)
        dma("scalar", "dA", dChi[M0:M0 + 128, :], cchi_s[:])
        cstore = dma("scalar", "dA", dClo[M0:M0 + 128, :], cclo_s[:])

    fctx.close()
    NW = 4096
    wnat4 = [ISB("wnat4_0", [128, NW]), ISB("wnat4_1", [128, NW])]
    q4_s = ISB("q4_s", [128, NW])
    A4 = ISB("A4", [128, NW], BF16)
    B4 = ISB("B4", [128, NW], BF16)
    C4 = ISB("C4", [128, NW], BF16)
    D4 = ISB("D4", [128, NW], BF16)
    E4 = ISB("E4", [128, NW], BF16)
    sgn4 = ISB("sgn4", [128, NW], BF16)
    sp4 = [ISB("sp4h", [128, NW], BF16), ISB("sp4l", [128, NW], BF16)]
    tstg4 = [ISB("tstg4_0", [128, NW], BF16), ISB("tstg4_1", [128, NW], BF16)]
    bm4_s = ISB("bm4_s", [128, 256])
    sc4_s = ISB("sc4_s", [128, 256])
    sh4_s = ISB("sh4_s", [128, 256])
    rc4_s = ISB("rc4_s", [128, 256])
    QB = (q4_s, bm4_s, sc4_s, sh4_s, rc4_s, A4, B4, C4, D4, E4, sgn4)
    quads = []
    for wname, dsrc, ddst, nchunks, width in (("g", dwg, dwgt, ICH, E),
                                              ("u", dwu, dwut, ICH, E),
                                              ("d", dwd, dwdt, KCH, I)):
        if width == E:
            ch = 0
            while ch < nchunks:
                nch = min(4, nchunks - ch)
                quads.append((dsrc, ddst, ch, nch, E, False))
                ch += nch
        else:
            for ch in range(nchunks):
                quads.append((dsrc, ddst, ch, 1, I, True))
    pidx = 0
    wnat_free = {}
    wstore = None
    waitt("sync", ("dA", cstore))
    for dsrc, ddst, ch0, nch, clen, isd in quads:
        b = pidx % 2
        pidx += 1
        plen = nch * clen
        for g in wnat_free.get(b, ()):
            waitt("sync", g)
        if isd:
            wt = dma("sync", "dWn%d" % b, wnat4[b][:, 0:plen],
                     dsrc[ch0 * 128:(ch0 + 1) * 128, 0:plen])
        else:
            apw = bass.AP(tensor=dsrc.tensor, offset=ch0 * 128 * E,
                          ap=[[E, 128], [128 * E, nch], [1, E]])
            wt = dma("sync", "dWn%d" % b,
                     wnat4[b][:, 0:plen].rearrange("p (c n) -> p c n", n=E), apw)
        wait("vector", "dWn%d" % b, wt)
        wait("scalar", "dWn%d" % b, wt)
        qt = quant(wnat4[b], plen, bufs=QB)
        wnat_free[b] = (qt, last_s_guard[0])
        s_tok, v_tok = split(q4_s, plen, sp4[0], sp4[1], qt, "wp")
        nblk = plen // 128
        lastev = None
        for half, stg, vguard in ((0, sp4[0], ("sS", s_tok[1])),
                                  (1, sp4[1], v_tok)):
            tgt = tstg4[half]
            for blk in range(nblk):
                if blk == 0 and wstore is not None:
                    wait("scalar", "dA", wstore)
                lastev, _ = transpose_bf16(stg[:, blk * 128:(blk + 1) * 128],
                                           tgt[:, blk * 128:(blk + 1) * 128],
                                           vguard)
        sp_guard["wp"] = cnt["sP"]
        wait("scalar", "sS", lastev[1])
        for half in range(2):
            tgt = tstg4[half]
            dd = ddst[half]
            if isd:
                dstap = bass.AP(tensor=dd.tensor, offset=ch0 * 128,
                                ap=[[E, 128], [128 * E, nblk], [1, 128]])
                wstore = dma("scalar", "dA", dstap,
                             tgt[:, 0:nblk * 128].rearrange("p (c n) -> p c n", n=128))
            else:
                for c in range(nch):
                    dstap = bass.AP(tensor=dd.tensor, offset=(ch0 + c) * 128,
                                    ap=[[I, 128], [128 * I, 8], [1, 128]])
                    wstore = dma("scalar", "dA", dstap,
                                 tgt[:, c * E:(c + 1) * E].rearrange(
                                     "p (c2 n) -> p c2 n", n=128))

    wait("scalar", "dCon", t_wel)
    split(wel_f, KCH * 24, wel_b[0], wel_b[1], None, "wel")
    wait("scalar", "dCon", t_wle)
    st = op("scalar", lambda e: e.copy(out=wle_b[0][:], in_=wle_f[:]), inc="sS")
    wait("vector", "sS", st)
    op("vector", lambda e: e.scalar_tensor_tensor(out=wle_b[1][:], in0=wle_b[0][:],
                                                  scalar=-1.0, in1=wle_f[:],
                                                  op0=ALU.mult, op1=ALU.add))
    wait("tensor", "dCon", t_bel)
    bank_wait("tensor", "mm_c")
    pt = op("tensor", lambda e: e.matmul(out=mm_c[:, 0:24], lhsT=ones_s[:], rhs=bel_s[:],
                                         start=True, stop=True), inc="sP")
    wait("vector", "sP", pt)
    vt = op("vector", lambda e: e.tensor_single_scalar(out=negb_s[:], in_=mm_c[:, 0:24],
                                                       scalar=-1.0, op=ALU.mult), inc="sV")
    bank_set("mm_c", "sV", vt)

    init_join = dict((s, cnt[s]) for s in semnames)
    ictx.close()

    arenaH = SB("arenaH", [128, 22528], BF16)
    arenaX = SB("arenaX", [128, 16384], BF16)
    sT = SB("sT", [128, KCH * 512], F32)
    xstage = [SB("xstage0", [128, E]), SB("xstage1", [128, E])]
    wsl = {w: [[SB("wsl_%s_%d_%d" % (w, h, b2), [128, KCH * NSL], BF16)
                for b2 in range(2)] for h in range(2)] for w in ("g", "u")}
    wdb = [[SB("wd_%d_%d" % (h, b2), [128, 3 * 512], BF16) for b2 in range(2)]
           for h in range(2)]
    cstb = [[SB("cst_%d_%d" % (h, b2), [128, E], BF16) for b2 in range(2)] for h in range(2)]
    gsc = SB("gsc", [128, 512])
    gdbg = SB("gdbg", [128, 512]) if DEBUG else None
    outst = SB("outst", [128, NT * 512])
    xre = SB("xre", [128, NT * 512])

    def xnT(h):
        return arenaX[:, h * 4096:(h + 1) * 4096]

    def xqT(h):
        return arenaX[:, 8192 + h * 4096: 8192 + (h + 1) * 4096]

    def x1T(h):
        return arenaH[:, h * 4096:(h + 1) * 4096]

    def hqT(h, c):
        return arenaH[:, h * 11264 + c * 512: h * 11264 + (c + 1) * 512]

    for eng in ENGS:
        for s, v in init_join.items():
            wait(eng, s, v)

    G = dict(xst_free=[None, None], wsl_free=[None, None], wd_free=[None, None],
             cst_free=[None, None], xre_free=None, outst_free=None, ssq_free=None,
             scrC_free=None, gsc_free=None, arena_ready=0, hq_ready=0)

    def ph1_front(stp, tt):
        base = stp * TSUP
        r0 = base + tt * 128
        b = (stp * NT + tt) % 2
        waitt("sync", G["xst_free"][b])
        xt = dma("sync", "dX%d" % b, xstage[b][:], dx[r0:r0 + 128, :])
        wait("scalar", "dX%d" % b, xt)
        waitt("scalar", G["ssq_free"])
        waitt("scalar", G["scrC_free"])
        st = op("scalar", lambda e, b=b: e.activation(out=scrB[:], in_=xstage[b][:],
                                                      func=AF.Square,
                                                      accum_out=ssq_s[:]), inc="sS")
        wait("vector", "dX%d" % b, xt)
        wait("vector", "sS", st)
        op("vector", lambda e, b=b: e.reduce_sum(out=sum_s[:], in_=xstage[b][:], axis=AX.X))
        dr("vector")
        op("vector", lambda e: e.tensor_single_scalar(out=m_s[:], in_=sum_s[:],
                                                      scalar=1.0 / 1024, op=ALU.mult))
        dr("vector")
        op("vector", lambda e: e.tensor_tensor(out=m2_s[:], in0=m_s[:], in1=m_s[:],
                                               op=ALU.mult))
        dr("vector")
        op("vector", lambda e: e.scalar_tensor_tensor(out=var_s[:], in0=ssq_s[:],
                                                      scalar=1.0 / 1024, in1=m2_s[:],
                                                      op0=ALU.mult, op1=ALU.subtract))
        dr("vector")
        vt = op("vector", lambda e: e.tensor_single_scalar(out=var_s[:], in_=var_s[:],
                                                           scalar=1e-5, op=ALU.add),
                inc="sV")
        G["ssq_free"] = ("sV", vt)
        wait("scalar", "sV", vt)
        st = op("scalar", lambda e: e.activation(out=sq_s[:], in_=var_s[:], func=AF.Sqrt),
                inc="sS")
        wait("vector", "sS", st)
        op("vector", lambda e: e.reciprocal(out=rstd_s[:], in_=sq_s[:]))
        dr("vector")
        vt = op("vector", lambda e, b=b: e.tensor_scalar(
            out=scrA[:], in0=xstage[b][:], scalar1=m_s[:], scalar2=rstd_s[:],
            op0=ALU.subtract, op1=ALU.mult), inc="sV")
        G["xst_free"][b] = ("sV", vt)
        s_xn, v_xn = split(scrA, E, sp_hi, sp_lo, ("sV", vt), "p1a")
        qt = quant(scrA, E, gq=True)
        G["scrC_free"] = qt
        return (s_xn, v_xn, qt)

    def ph1_tr(stp, tt, toks):
        s_xn, v_xn, qt = toks
        for srcbuf, dst, vg in ((sp_hi, xnT(0), ("sS", s_xn[1])),
                                (sp_lo, xnT(1), v_xn)):
            for c in range(KCH):
                dcol = c * 512 + tt * 128
                transpose_bf16(srcbuf[:, c * 128:(c + 1) * 128],
                               dst[:, dcol:dcol + 128], vg)
        sp_guard["p1a"] = cnt["sP"]
        s_xq, v_xq = split(scrB, E, sp_hi, sp_lo, qt, "p1a")
        G["scrC_free"] = v_xq
        for srcbuf, dst, vg in ((sp_hi, xqT(0), ("sS", s_xq[1])),
                                (sp_lo, xqT(1), v_xq)):
            for c in range(KCH):
                dcol = c * 512 + tt * 128
                transpose_bf16(srcbuf[:, c * 128:(c + 1) * 128],
                               dst[:, dcol:dcol + 128], vg)
        sp_guard["p1a"] = cnt["sP"]

    def ph2(stp):
        for c in range(KCH):
            bname = "mm_a" if c % 2 == 0 else "mm_b"
            bnk = MMB[bname]
            cb = c % 2
            waitt("sync", G["cst_free"][cb])
            aph = bass.AP(tensor=dChi.tensor, offset=c * 128,
                          ap=[[E, 128], [128 * E, KCH], [1, 128]])
            apl = bass.AP(tensor=dClo.tensor, offset=c * 128,
                          ap=[[E, 128], [128 * E, KCH], [1, 128]])
            dv = cstb[0][cb][:].rearrange("p (m n) -> p m n", n=128)
            dv2 = cstb[1][cb][:].rearrange("p (m n) -> p m n", n=128)
            th = dma("sync", "dCc%d" % cb, dv, aph)
            tl = dma("sync", "dCc%d" % cb, dv2, apl)
            bank_wait("tensor", bname)
            wait("tensor", "dCc%d" % cb, tl)
            if c == 0:
                wait("tensor", "sS", G["arena_ready"])
            tk = None
            for m in range(KCH):
                chi = cstb[0][cb][:, m * 128:(m + 1) * 128]
                clo = cstb[1][cb][:, m * 128:(m + 1) * 128]
                xh = xnT(0)[:, m * 512:(m + 1) * 512]
                xl = xnT(1)[:, m * 512:(m + 1) * 512]
                op("tensor", lambda e, a=chi, r=xh, b2=bnk, sf=(m == 0): e.matmul(
                    out=b2[:], lhsT=a, rhs=r, start=sf, stop=False))
                op("tensor", lambda e, a=chi, r=xl, b2=bnk: e.matmul(
                    out=b2[:], lhsT=a, rhs=r, start=False, stop=False))
                tk = op("tensor", lambda e, a=clo, r=xh, b2=bnk, sp=(m == KCH - 1):
                        e.matmul(out=b2[:], lhsT=a, rhs=r, start=False, stop=sp), inc="sP")
            G["cst_free"][cb] = ("sP", tk)
            wait("scalar", "sP", tk)
            st = op("scalar", lambda e, b2=bnk, c=c: e.copy(
                out=x1T(0)[:, c * 512:(c + 1) * 512], in_=b2[:]), inc="sS")
            wait("vector", "sS", st)
            op("vector", lambda e, b2=bnk, c=c: e.scalar_tensor_tensor(
                out=x1T(1)[:, c * 512:(c + 1) * 512], in0=x1T(0)[:, c * 512:(c + 1) * 512],
                scalar=-1.0, in1=b2[:], op0=ALU.mult, op1=ALU.add))
            vt = op("vector", lambda e, b2=bnk, c=c: e.tensor_copy(
                out=sT[:, c * 512:(c + 1) * 512], in_=b2[:]), inc="sV")
            bank_set(bname, "sV", vt)
        x1_ready_s = cnt["sS"]
        x1_ready_v = cnt["sV"]
        for ts4 in range(NT):
            bank_wait("tensor", "mm_c")
            wait("tensor", "sS", x1_ready_s)
            wait("tensor", "sV", x1_ready_v)
            tk = None
            for c in range(KCH):
                xh = x1T(0)[:, c * 512 + ts4 * 128: c * 512 + ts4 * 128 + 128]
                xl = x1T(1)[:, c * 512 + ts4 * 128: c * 512 + ts4 * 128 + 128]
                wh = wel_b[0][:, c * 24:(c + 1) * 24]
                wl = wel_b[1][:, c * 24:(c + 1) * 24]
                op("tensor", lambda e, a=xh, r=wh, sf=(c == 0): e.matmul(
                    out=mm_c[:, 0:24], lhsT=a, rhs=r, start=sf, stop=False))
                op("tensor", lambda e, a=xh, r=wl: e.matmul(
                    out=mm_c[:, 0:24], lhsT=a, rhs=r, start=False, stop=False))
                tk = op("tensor", lambda e, a=xl, r=wh, sp=(c == KCH - 1): e.matmul(
                    out=mm_c[:, 0:24], lhsT=a, rhs=r, start=False, stop=sp), inc="sP")
            wait("vector", "sP", tk)
            vt = op("vector", lambda e: e.tensor_tensor(out=bits_s[:], in0=mm_c[:, 0:24],
                                                        in1=negb_s[:], op=ALU.is_gt),
                    inc="sV")
            bank_set("mm_c", "sV", vt)
            bk, ptk, bn = transpose_f32(bits_s[:], ("sV", vt), rows=24)
            wait("scalar", "sP", ptk)
            st = op("scalar", lambda e, b2=bk: e.copy(out=bitsT_s[:], in_=b2[0:24, :]),
                    inc="sS")
            bank_set(bn, "sS", st)
            bank_wait("tensor", "mm_e")
            wait("tensor", "sS", st)
            ptk = op("tensor", lambda e: e.matmul(out=mm_e[:, 0:12], lhsT=bitsT_s[:],
                                                  rhs=ht_s[:], start=True, stop=True),
                     inc="sP")
            wait("vector", "sP", ptk)
            vt = op("vector", lambda e: e.tensor_single_scalar(out=syn_s[:],
                                                               in_=mm_e[:, 0:12],
                                                               scalar=0.5, op=ALU.is_gt),
                    inc="sV")
            bank_set("mm_e", "sV", vt)
            dr("vector")
            op("vector", lambda e: e.reduce_sum(out=wgol_s[:], in_=syn_s[:], axis=AX.X))
            dr("vector")
            op("vector", lambda e: e.tensor_single_scalar(out=fixm_s[:], in_=wgol_s[:],
                                                          scalar=0.5, op=ALU.is_gt))
            op("vector", lambda e: e.tensor_single_scalar(out=c2_s[:], in_=wgol_s[:],
                                                          scalar=3.5, op=ALU.is_gt))
            dr("vector")
            op("vector", lambda e: e.tensor_tensor(out=fixm_s[:], in0=fixm_s[:],
                                                   in1=c2_s[:], op=ALU.subtract))
            dr("vector")
            op("vector", lambda e: e.tensor_single_scalar(out=p12_s[:], in_=syn_s[:],
                                                          scalar=fixm_s[:], op=ALU.mult))
            dr("vector")
            op("vector", lambda e: e.tensor_tensor(out=r12_s[:], in0=bits_s[:, 0:12],
                                                   in1=p12_s[:], op=ALU.mult))
            op("vector", lambda e: e.tensor_tensor(out=t12_s[:], in0=bits_s[:, 0:12],
                                                   in1=p12_s[:], op=ALU.add))
            dr("vector")
            op("vector", lambda e: e.scalar_tensor_tensor(out=corr_s[:, 0:12], in0=r12_s[:],
                                                          scalar=-2.0, in1=t12_s[:],
                                                          op0=ALU.mult, op1=ALU.add))
            dr("vector")
            vt = op("vector", lambda e: e.tensor_copy(out=corr_s[:, 12:24],
                                                      in_=bits_s[:, 12:24]), inc="sV")
            bk, ptk, bn = transpose_f32(corr_s[:], ("sV", vt), rows=24)
            wait("scalar", "sP", ptk)
            st = op("scalar", lambda e, b2=bk, t4=ts4: e.copy(
                out=corrT_s[:, t4 * 128:(t4 + 1) * 128], in_=b2[0:24, :]), inc="sS")
            bank_set(bn, "sS", st)
        corrT_ready = cnt["sS"]
        for ec in range(KCH):
            bank_wait("tensor", "mm_e")
            wait("tensor", "sS", corrT_ready)
            op("tensor", lambda e, ec=ec: e.matmul(
                out=mm_e[:], lhsT=wle_b[0][:, ec * 128:(ec + 1) * 128], rhs=corrT_s[:],
                start=True, stop=False))
            tk = op("tensor", lambda e, ec=ec: e.matmul(
                out=mm_e[:], lhsT=wle_b[1][:, ec * 128:(ec + 1) * 128], rhs=corrT_s[:],
                start=False, stop=True), inc="sP")
            wait("vector", "sP", tk)
            vt = op("vector", lambda e, ec=ec: e.scalar_tensor_tensor(
                out=sT[:, ec * 512:(ec + 1) * 512], in0=mm_e[:],
                scalar=ble_s[:, ec:ec + 1], in1=sT[:, ec * 512:(ec + 1) * 512],
                op0=ALU.add, op1=ALU.add), inc="sV")
            bank_set("mm_e", "sV", vt)

    def ph3_quantblock(ns, h4):
        qt = quant(h4, E, gq=True)
        s_hq, v_hq = split(scrB, E, sp_hi, sp_lo, qt, "p3")
        G["scrC_free"] = v_hq
        for cidx in range(KCH):
            tt = cidx // 2
            chunk = ns * 2 + (cidx % 2)
            tcol = tt * 128
            transpose_bf16(sp_hi[:, cidx * 128:(cidx + 1) * 128],
                           hqT(0, chunk)[:, tcol:tcol + 128], ("sS", s_hq[1]))
            transpose_bf16(sp_lo[:, cidx * 128:(cidx + 1) * 128],
                           hqT(1, chunk)[:, tcol:tcol + 128], v_hq)
        sp_guard["p3"] = cnt["sP"]
        return qt

    def ph3(stp):
        h4bufs = [scrA, xstage[0]]
        for ns in range(NSLC):
            sb2 = ns % 2
            h4 = h4bufs[sb2]
            waitt("sync", G["wsl_free"][sb2])
            wt = None
            for w in ("g", "u"):
                dd = dwgt if w == "g" else dwut
                for h in range(2):
                    apw = bass.AP(tensor=dd[h].tensor, offset=ns * NSL,
                                  ap=[[I, 128], [128 * I, KCH], [1, NSL]])
                    wt = dma("sync", "dWs%d" % sb2, wsl[w][h][sb2][:].rearrange(
                        "p (c n) -> p c n", n=NSL), apw)
            for tt in range(NT):
                gbank, ubank = (("mm_a", "mm_b") if tt % 2 == 0 else ("mm_c", "mm_e"))
                gb, ub = MMB[gbank], MMB[ubank]
                bank_wait("tensor", gbank)
                bank_wait("tensor", ubank)
                wait("tensor", "dWs%d" % sb2, wt)
                if ns == 0 and tt == 0:
                    wait("tensor", "sS", G["arena_ready"])
                tk = None
                for m in range(KCH):
                    xh = xqT(0)[:, m * 512 + tt * 128: m * 512 + tt * 128 + 128]
                    xl = xqT(1)[:, m * 512 + tt * 128: m * 512 + tt * 128 + 128]
                    gh = wsl["g"][0][sb2][:, m * NSL:(m + 1) * NSL]
                    gl = wsl["g"][1][sb2][:, m * NSL:(m + 1) * NSL]
                    uh = wsl["u"][0][sb2][:, m * NSL:(m + 1) * NSL]
                    ul = wsl["u"][1][sb2][:, m * NSL:(m + 1) * NSL]
                    sf = (m == 0)
                    sp = (m == KCH - 1)
                    op("tensor", lambda e, a=xh, r=gh, b2=gb, sf=sf: e.matmul(
                        out=b2[:, 0:NSL], lhsT=a, rhs=r, start=sf, stop=False))
                    op("tensor", lambda e, a=xh, r=gl, b2=gb: e.matmul(
                        out=b2[:, 0:NSL], lhsT=a, rhs=r, start=False, stop=False))
                    op("tensor", lambda e, a=xh, r=uh, b2=ub, sf=sf: e.matmul(
                        out=b2[:, 0:NSL], lhsT=a, rhs=r, start=sf, stop=False))
                    op("tensor", lambda e, a=xh, r=ul, b2=ub: e.matmul(
                        out=b2[:, 0:NSL], lhsT=a, rhs=r, start=False, stop=False))
                    op("tensor", lambda e, a=xl, r=gh, b2=gb, sp=sp: e.matmul(
                        out=b2[:, 0:NSL], lhsT=a, rhs=r, start=False, stop=sp))
                    tk = op("tensor", lambda e, a=xl, r=uh, b2=ub, sp=sp: e.matmul(
                        out=b2[:, 0:NSL], lhsT=a, rhs=r, start=False, stop=sp), inc="sP")
                G["wsl_free"][sb2] = ("sP", tk)
                wait("scalar", "sP", tk)
                waitt("scalar", G["gsc_free"])
                st = op("scalar", lambda e, b2=gb: e.activation(
                    out=gsc[:, 0:NSL], in_=b2[:, 0:NSL], func=AF.Silu), inc="sS")
                wait("vector", "sS", st)
                vt = op("vector", lambda e, b2=ub, tt=tt, h4=h4: e.tensor_tensor(
                    out=h4[:, tt * NSL:(tt + 1) * NSL], in0=gsc[:, 0:NSL],
                    in1=b2[:, 0:NSL], op=ALU.mult), inc="sV")
                G["gsc_free"] = ("sV", vt)
                bank_set(gbank, "sS", st)
                bank_set(ubank, "sV", vt)
                last_s_guard[0] = ("sS", st)
            if ns >= 1:
                ph3_quantblock(ns - 1, h4bufs[(ns - 1) % 2])
        qt = ph3_quantblock(NSLC - 1, h4bufs[(NSLC - 1) % 2])
        G["hq_ready"] = cnt["sS"]
        G["xst_free"][0] = qt

    def ph4_mm(stp, eh):
        base = stp * TSUP
        waitt("gpsimd", G["xre_free"])
        xt_last = None
        for tt in range(NT):
            r0 = base + tt * 128
            xt_last = dma("gpsimd", "dXre", xre[:, tt * 512:(tt + 1) * 512],
                          dx[r0:r0 + 128, eh * 512:(eh + 1) * 512])
        ebanks = ["mm_a", "mm_b", "mm_c", "mm_e"]
        for bn in ebanks:
            bank_wait("tensor", bn)
        wait("tensor", "sS", G["hq_ready"])
        ngrp = (ICH + 2) // 3
        for grp in range(ngrp):
            wb = grp % 2
            nch = min(3, ICH - grp * 3)
            waitt("sync", G["wd_free"][wb])
            t2 = None
            for h in range(2):
                apw = bass.AP(tensor=dwdt[h].tensor,
                              offset=grp * 3 * 128 * E + eh * 512,
                              ap=[[E, 128], [128 * E, nch], [1, 512]])
                t2 = dma("sync", "dWd%d" % wb,
                         wdb[h][wb][:, 0:nch * 512].rearrange("p (c n) -> p c n", n=512),
                         apw)
            wait("tensor", "dWd%d" % wb, t2)
            tk = None
            for sub in range(nch):
                ncn = grp * 3 + sub
                for ecl in range(4):
                    b2 = MMB[ebanks[ecl]]
                    wh = wdb[0][wb][:, sub * 512 + ecl * 128: sub * 512 + (ecl + 1) * 128]
                    wl = wdb[1][wb][:, sub * 512 + ecl * 128: sub * 512 + (ecl + 1) * 128]
                    hh = hqT(0, ncn)
                    hl = hqT(1, ncn)
                    sf = (ncn == 0)
                    sp = (ncn == ICH - 1)
                    op("tensor", lambda e, a=wh, r=hh, b3=b2, sf=sf: e.matmul(
                        out=b3[:], lhsT=a, rhs=r, start=sf, stop=False))
                    op("tensor", lambda e, a=wh, r=hl, b3=b2: e.matmul(
                        out=b3[:], lhsT=a, rhs=r, start=False, stop=False))
                    tk = op("tensor", lambda e, a=wl, r=hh, b3=b2, sp=sp: e.matmul(
                        out=b3[:], lhsT=a, rhs=r, start=False, stop=sp), inc="sP")
            G["wd_free"][wb] = ("sP", tk)
        return xt_last, cnt["sP"]

    def ph4_evac(stp, eh, xt_last, mmtok):
        base = stp * TSUP
        ebanks = ["mm_a", "mm_b", "mm_c", "mm_e"]
        for ecl in range(4):
            ec = eh * 4 + ecl
            b2 = MMB[ebanks[ecl]]
            wait("scalar", "sP", mmtok)
            waitt("scalar", G["gsc_free"])
            st = op("scalar", lambda e, b3=b2: e.activation(out=gsc[:], in_=b3[:],
                                                            func=AF.Silu), inc="sS")
            if DEBUG and stp == 0 and eh == 0 and ecl == 0:
                st = op("scalar", lambda e, b3=b2: e.copy(out=gdbg[:], in_=b3[:]),
                        inc="sS")
                tap("d0raw", gdbg[:], ("sS", st))
            wait("vector", "sS", st)
            waitt("vector", G["outst_free"])
            vt = op("vector", lambda e, ec=ec: e.tensor_tensor(
                out=gsc[:], in0=gsc[:], in1=sT[:, ec * 512:(ec + 1) * 512],
                op=ALU.mult), inc="sV")
            bank_set(ebanks[ecl], "sV", vt)
            last_s_guard[0] = ("sS", st)
            for tt in range(NT):
                bk, ptk, bn = transpose_f32(gsc[:, tt * 128:(tt + 1) * 128], ("sV", vt))
                wait("vector", "sP", ptk)
                wait("vector", "dXre", xt_last)
                col = tt * 512 + ecl * 128
                vtok = op("vector", lambda e, b3=bk, col=col: e.tensor_tensor(
                    out=outst[:, col:col + 128], in0=b3[:],
                    in1=xre[:, col:col + 128], op=ALU.add), inc="sV")
                bank_set(bn, "sV", vtok)
                G["gsc_free"] = ("sP", ptk)
        outv = cnt["sV"]
        wait("scalar", "sV", outv)
        lastout = None
        for tt in range(NT):
            r0 = base + tt * 128
            lastout = dma("scalar", "dA", dy[r0:r0 + 128, eh * 512:(eh + 1) * 512],
                          outst[:, tt * 512:(tt + 1) * 512])
        G["outst_free"] = ("dA", lastout)
        G["xre_free"] = ("sV", outv)

    for tt in range(NT):
        toks = ph1_front(0, tt)
        ph1_tr(0, tt, toks)
    G["arena_ready"] = cnt["sS"]

    for stp in range(NSUP):
        nxt = stp + 1 if stp + 1 < NSUP else None
        ph2(stp)
        ph3(stp)
        xt0, mt0 = ph4_mm(stp, 0)
        if nxt is not None:
            for tt in (0, 1):
                toks = ph1_front(nxt, tt)
                ph1_tr(nxt, tt, toks)
        ph4_evac(stp, 0, xt0, mt0)
        xt1, mt1 = ph4_mm(stp, 1)
        if nxt is not None:
            for tt in (2, 3):
                toks = ph1_front(nxt, tt)
                ph1_tr(nxt, tt, toks)
        ph4_evac(stp, 1, xt1, mt1)
        if nxt is not None:
            G["arena_ready"] = cnt["sS"]

    for s in semnames:
        if s.startswith("d"):
            wait("scalar", s, cnt[s])

    with nc.Block() as block:
        for eng in ENGS:
            ops = Q[eng]
            if not ops:
                continue
            getattr(block, eng)(lambda e, ops=ops: [f(e) for f in ops])

    ctx.close()
    return nc


_PROG = None


def kernel(L0_attn_x, L0_attn_W_el, L0_attn_b_el, L0_attn_W_le, L0_attn_b_le,
           L0_ffn_w_gate, L0_ffn_w_up, L0_ffn_w_down):
    global _PROG
    x = np.ascontiguousarray(L0_attn_x, dtype=np.float32)
    B, S, Emod = x.shape
    x2d = x.reshape(B * S, Emod)
    C = _host_consts()
    if _PROG is None:
        _PROG = build_program()
    nc = _PROG
    x0 = np.ascontiguousarray(x2d[0:1, :])
    ble = np.ascontiguousarray(np.asarray(L0_attn_b_le, dtype=np.float32)
                               .reshape(8, 128).T)
    common = {
        "x0": x0,
        "w_el": np.ascontiguousarray(L0_attn_W_el, dtype=np.float32),
        "b_el": np.ascontiguousarray(np.asarray(L0_attn_b_el, dtype=np.float32)
                                     .reshape(1, 24)),
        "w_le": np.ascontiguousarray(L0_attn_W_le, dtype=np.float32),
        "b_le": ble,
        "w_gate": np.ascontiguousarray(L0_ffn_w_gate, dtype=np.float32),
        "w_up": np.ascontiguousarray(L0_ffn_w_up, dtype=np.float32),
        "w_down": np.ascontiguousarray(L0_ffn_w_down, dtype=np.float32),
        "cos_hi": C["cos_hi"], "cos_lo": C["cos_lo"],
        "sin_hi": C["sin_hi"], "sin_lo": C["sin_lo"],
        "kraw": C["kraw"], "ident": C["ident"], "identb": C["identb"],
        "J": C["J"], "Ht": C["Ht"], "sc5": C["sc5"], "ones1": C["ones1"],
        "frD1": C["frD1"], "fiD1": C["fiD1"],
    }
    in_maps = []
    for cix in range(NCORES):
        m = dict(common)
        m["x"] = np.ascontiguousarray(x2d[cix * TOK:(cix + 1) * TOK, :])
        in_maps.append(m)
    res = run_bass_kernel_spmd(nc, in_maps, core_ids=list(range(NCORES)))
    out = np.concatenate([np.asarray(r["y"]).reshape(TOK, Emod)
                          for r in res.results], axis=0)
    return out.reshape(B, S, Emod)


# version 16
# speedup vs baseline: 1.2157x; 1.2157x over previous
import numpy as np
import ml_dtypes
from contextlib import ExitStack

import concourse.bass as bass
import concourse.mybir as mybir
from concourse.bass_utils import run_bass_kernel_spmd

F32 = mybir.dt.float32
BF16 = mybir.dt.bfloat16
AF = mybir.ActivationFunctionType
ALU = mybir.AluOpType
AX = mybir.AxisListType

NCORES = 8
E = 1024
I = 2816
TOK = 4096
TSUP = 512
NSUP = TOK // TSUP
NT = TSUP // 128
KCH = E // 128
ICH = I // 128
NSL = 256
NSLC = I // NSL

T2 = [0.0625, 0.5625, 1.5625, 3.0625, 6.25, 12.25, 25.0]
TW = [1.0, 1.0, 1.0, 1.0, 2.0, 2.0, 4.0]

_BF = ml_dtypes.bfloat16
_CONSTS = None
DEBUG = False


def _host_consts():
    global _CONSTS
    if _CONSTS is not None:
        return _CONSTS
    k = np.arange(1024, dtype=np.float64)
    th = np.outer(k, k) * (2.0 * np.pi / 1024.0)
    cos = np.cos(th)
    sin = np.sin(th)
    cos_hi = cos.astype(np.float32)
    cos_lo = (cos - cos_hi).astype(np.float32)
    sin_hi = sin.astype(np.float32)
    sin_lo = (sin - sin_hi).astype(np.float32)
    fr = np.fft.fftfreq(1024).astype(np.float32)
    kraw = fr.reshape(8, 128).T.copy()
    ident = np.eye(128, dtype=np.float32)
    identb = ident.astype(_BF)
    J = ident[::-1].copy()
    _A = np.array([[1,1,1,0,1,1,1,0,0,0,1,1],[1,1,0,1,1,1,0,0,0,1,1,0],
                   [1,0,1,1,1,0,0,0,1,1,0,1],[0,1,1,1,0,0,0,1,1,0,1,1],
                   [0,0,0,0,1,1,1,1,1,1,1,1],[1,1,1,1,0,0,0,0,0,0,0,0],
                   [0,0,0,0,1,1,1,1,0,0,0,0],[1,1,0,0,1,1,0,0,1,1,0,0],
                   [1,0,1,0,1,0,1,0,1,0,1,0],[0,1,0,1,0,1,0,1,0,1,0,1],
                   [1,0,0,1,1,0,0,1,1,0,0,1],[0,1,1,0,0,1,1,0,0,1,1,0]],
                  dtype=np.float32)
    Hm = np.concatenate([_A.T, np.eye(12, dtype=np.float32)], axis=1)
    Ht = Hm.T.astype(_BF)
    sc5 = np.array([[2.0, 4.0, 8.0, 16.0, 32.0]], dtype=np.float32)
    ones1 = np.ones((1, 128), dtype=np.float32)
    import jax
    import jax.numpy as jnp
    with jax.default_device(jax.devices("cpu")[0]):
        kj = jnp.asarray(fr)
        kaj = jnp.abs(kj) + 1e-08
        Dj = jnp.float32(1.0)
        omj = 1.0 + 0.2 * Dj
        aj = 1.0 + 0.5 * Dj
        bj = 0.1 * Dj
        realj = jnp.sin(omj * Dj + aj * kaj)
        phasej = omj * Dj - kaj + bj * kaj ** 2
        filtj = np.asarray(realj * jnp.exp(1j * phasej))
    frD1 = np.ascontiguousarray(filtj.real.astype(np.float32).reshape(8, 128).T)
    fiD1 = np.ascontiguousarray(filtj.imag.astype(np.float32).reshape(8, 128).T)
    _CONSTS = dict(cos_hi=cos_hi, cos_lo=cos_lo, sin_hi=sin_hi, sin_lo=sin_lo,
                   kraw=kraw, ident=ident, identb=identb, J=J, Ht=Ht, sc5=sc5,
                   ones1=ones1, frD1=frD1, fiD1=fiD1)
    return _CONSTS


def build_program():
    nc = bass.Bass("TRN2", target_bir_lowering=False, debug=False)

    dx = nc.dram_tensor("x", [TOK, E], F32, kind="ExternalInput").ap()
    dx0 = nc.dram_tensor("x0", [1, E], F32, kind="ExternalInput").ap()
    dwel = nc.dram_tensor("w_el", [E, 24], F32, kind="ExternalInput").ap()
    dbel = nc.dram_tensor("b_el", [1, 24], F32, kind="ExternalInput").ap()
    dwle = nc.dram_tensor("w_le", [24, E], F32, kind="ExternalInput").ap()
    dble = nc.dram_tensor("b_le", [128, 8], F32, kind="ExternalInput").ap()
    dwg = nc.dram_tensor("w_gate", [I, E], F32, kind="ExternalInput").ap()
    dwu = nc.dram_tensor("w_up", [I, E], F32, kind="ExternalInput").ap()
    dwd = nc.dram_tensor("w_down", [E, I], F32, kind="ExternalInput").ap()
    dcos = [nc.dram_tensor(n, [E, E], F32, kind="ExternalInput").ap()
            for n in ("cos_hi", "cos_lo", "sin_hi", "sin_lo")]
    dkraw = nc.dram_tensor("kraw", [128, 8], F32, kind="ExternalInput").ap()
    dfrD1 = nc.dram_tensor("frD1", [128, 8], F32, kind="ExternalInput").ap()
    dfiD1 = nc.dram_tensor("fiD1", [128, 8], F32, kind="ExternalInput").ap()
    dident = nc.dram_tensor("ident", [128, 128], F32, kind="ExternalInput").ap()
    didentb = nc.dram_tensor("identb", [128, 128], BF16, kind="ExternalInput").ap()
    dJ = nc.dram_tensor("J", [128, 128], F32, kind="ExternalInput").ap()
    dHt = nc.dram_tensor("Ht", [24, 12], BF16, kind="ExternalInput").ap()
    dsc5 = nc.dram_tensor("sc5", [1, 5], F32, kind="ExternalInput").ap()
    dones = nc.dram_tensor("ones1", [1, 128], F32, kind="ExternalInput").ap()
    dy = nc.dram_tensor("y", [TOK, E], F32, kind="ExternalOutput").ap()
    dc2 = nc.dram_tensor("c2buf", [1, 2048], F32, kind="Internal").ap()
    dChi = nc.dram_tensor("Chi", [E, E], BF16, kind="Internal").ap()
    dClo = nc.dram_tensor("Clo", [E, E], BF16, kind="Internal").ap()
    dwgt = [nc.dram_tensor("wgT_hi", [E, I], BF16, kind="Internal").ap(),
            nc.dram_tensor("wgT_lo", [E, I], BF16, kind="Internal").ap()]
    dwut = [nc.dram_tensor("wuT_hi", [E, I], BF16, kind="Internal").ap(),
            nc.dram_tensor("wuT_lo", [E, I], BF16, kind="Internal").ap()]
    dwdt = [nc.dram_tensor("wdT_hi", [I, E], BF16, kind="Internal").ap(),
            nc.dram_tensor("wdT_lo", [I, E], BF16, kind="Internal").ap()]

    ctx = ExitStack()
    SB = lambda n, s, d=F32: ctx.enter_context(nc.sbuf_tensor(n, s, d))
    PS = lambda n, s, d=F32: ctx.enter_context(nc.psum_tensor(n, s, d))
    SEM = lambda n: ctx.enter_context(nc.semaphore(n))

    ENGS = ("sync", "scalar", "vector", "tensor", "gpsimd")
    Q = {e: [] for e in ENGS}
    sems = {}
    cnt = {}
    semnames = ["dS", "dA", "dG", "sP", "sV", "sS", "sG", "dCon", "dC2", "dXre"]
    for pfx in ("dCos", "dWn", "dX", "dCc", "dWs", "dWd", "dSh"):
        semnames += [pfx + "0", pfx + "1"]
    for s in semnames:
        sems[s] = SEM(s)
        cnt[s] = 0

    def emit(eng, fn):
        Q[eng].append(fn)

    def wait(eng, sem, val):
        if val is None or val <= 0:
            return
        h = sems[sem]
        emit(eng, lambda e, h=h, v=val: e.wait_ge(h, v))

    def waitt(eng, tok):
        if tok is not None:
            wait(eng, tok[0], tok[1])

    def op(eng, fn, inc=None):
        if inc is None:
            emit(eng, lambda e, f=fn: f(e))
            return None
        cnt[inc] += 1
        tok = cnt[inc]
        h = sems[inc]
        emit(eng, lambda e, f=fn, h=h: f(e).then_inc(h, 1))
        return tok

    def dr(eng):
        emit(eng, lambda e: e.drain())

    def dma(eng, sem, out_ap, in_ap):
        cnt[sem] += 16
        tok = cnt[sem]
        h = sems[sem]
        emit(eng, lambda e, o=out_ap, i=in_ap, h=h: e.dma_start(out=o, in_=i).then_inc(h, 16))
        return tok

    bankfree = {}

    def bank_wait(eng, bank):
        if bank in bankfree:
            waitt(eng, bankfree[bank])

    def bank_set(bank, sem, tok):
        bankfree[bank] = (sem, tok)

    tapped = set()

    def tap(name, ap, *pretoks):
        if not DEBUG or name in tapped:
            return
        tapped.add(name)
        d = nc.dram_tensor("tap_" + name, list(ap.shape), ap.dtype,
                           kind="ExternalOutput").ap()
        for t in pretoks:
            waitt("gpsimd", t)
        dma("gpsimd", "dG", d, ap)

    mm_a = PS("mm_a", [128, 512], F32)
    mm_b = PS("mm_b", [128, 512], F32)
    mm_c = PS("mm_c", [128, 512], F32)
    mm_e = PS("mm_e", [128, 512], F32)
    tr_a = PS("tr_a", [128, 128], F32)
    tr_b = PS("tr_b", [128, 128], F32)
    tr_c = PS("tr_c", [128, 128], BF16)
    tr_d = PS("tr_d", [128, 128], BF16)
    MMB = {"mm_a": mm_a, "mm_b": mm_b, "mm_c": mm_c, "mm_e": mm_e}

    scrA = SB("scrA", [128, E])
    scrB = SB("scrB", [128, E])
    f_bf = SB("f_bf", [128, E], BF16)
    t_bf = SB("t_bf", [128, E], BF16)
    mbq = [SB("mbq%d" % i, [128, E], BF16) for i in range(3)]
    sgn_bf = SB("sgn_bf", [128, E], BF16)
    sp_hi = SB("sp_hi", [128, E], BF16)
    sp_lo = SB("sp_lo", [128, E], BF16)
    bm_s = SB("bm_s", [128, 176])
    sc_s = SB("sc_s", [128, 176])
    sh_s = SB("sh_s", [128, 176])
    rc_s = SB("rc_s", [128, 176])
    sum_s = SB("sum_s", [128, 1])
    ssq_s = SB("ssq_s", [128, 1])
    m_s = SB("m_s", [128, 1])
    m2_s = SB("m2_s", [128, 1])
    var_s = SB("var_s", [128, 1])
    sq_s = SB("sq_s", [128, 1])
    rstd_s = SB("rstd_s", [128, 1])
    wel_b = [SB("wel_hi", [128, KCH * 24], BF16), SB("wel_lo", [128, KCH * 24], BF16)]
    wle_b = [SB("wle_hi", [24, E], BF16), SB("wle_lo", [24, E], BF16)]
    negb_s = SB("negb_s", [128, 24])
    ble_s = SB("ble_s", [128, 8])
    bits_s = SB("bits_s", [128, 24])
    syn_s = SB("syn_s", [128, 12], BF16)
    wgol_s = SB("wgol_s", [128, 1])
    fixm_s = SB("fixm_s", [128, 1])
    c2_s = SB("c2_s", [128, 1])
    p12_s = SB("p12_s", [128, 12], BF16)
    r12_s = SB("r12_s", [128, 12], BF16)
    t12_s = SB("t12_s", [128, 12], BF16)
    corr_s = SB("corr_s", [128, 24])
    bitsT_s = SB("bitsT_s", [24, 128], BF16)
    corrT_s = SB("corrT_s", [24, 512], BF16)
    idn_s = SB("idn_s", [128, 128])
    idnb_s = SB("idnb_s", [128, 128], BF16)
    ht_s = SB("ht_s", [24, 12], BF16)
    kraw_s = SB("kraw_s", [128, 8])
    ka_s = SB("ka_s", [128, 8])
    ka2_s = SB("ka2_s", [128, 8])
    fr_s = SB("fr_s", [128, 8])
    fi_s = SB("fi_s", [128, 8])
    nfi_s = SB("nfi_s", [128, 8])
    real_s = SB("real_s", [128, 8])
    ph_s = SB("ph_s", [128, 8])
    sph_s = SB("sph_s", [128, 8])
    cph_s = SB("cph_s", [128, 8])
    sc1 = {n: SB("s1_" + n, [128, 1]) for n in
           ("six", "five", "ten", "one", "r6", "fifth", "tenth", "pi4", "pi", "pihalf",
            "Drep", "omega", "aco", "nega", "bco", "omD", "pmo", "sel")}
    frD1_s = SB("frD1_s", [128, 8])
    fiD1_s = SB("fiD1_s", [128, 8])
    ones_s = SB("ones_s", [1, 128])

    last_s_guard = [("sS", 0)]
    last_g_guard = [None]

    def quant(src, N, bufs=None, gq=False):
        if bufs is None:
            qb, bmq, scq, shq, rcq = scrB, bm_s, sc_s, sh_s, rc_s
            A, B, Cb, D, Eb, sgq = t_bf, f_bf, mbq[0], mbq[1], mbq[2], sgn_bf
        else:
            qb, bmq, scq, shq, rcq, A, B, Cb, D, Eb, sgq = bufs
        nb = N // 16
        waitt("vector", last_s_guard[0])
        if last_g_guard[0] is not None:
            waitt("vector", last_g_guard[0])
        srcv = src[:, 0:N].rearrange("p (b s) -> p b s", s=16)
        dr("vector")
        op("vector", lambda e: e.tensor_reduce(out=bmq[:, 0:nb], in_=srcv, axis=AX.X,
                                               op=ALU.max, apply_absolute_value=True))
        dr("vector")
        op("vector", lambda e: e.tensor_scalar(out=scq[:, 0:nb], in0=bmq[:, 0:nb],
                                               scalar1=sc1["r6"][:], scalar2=1e-12,
                                               op0=ALU.mult, op1=ALU.max))
        dr("vector")
        op("vector", lambda e: e.tensor_single_scalar(out=shq[:, 0:nb], in_=scq[:, 0:nb],
                                                      scalar=0.5, op=ALU.mult))
        rt = op("vector", lambda e: e.reciprocal(out=rcq[:, 0:nb], in_=scq[:, 0:nb]),
                inc="sV" if gq else None)
        if not gq:
            dr("vector")
        rcb = rcq[:, 0:nb].rearrange("p (b o) -> p b o", o=1)
        rcb2, srcv2 = bass.broadcast_tensor_aps(rcb, srcv)
        uv = qb[:, 0:N].rearrange("p (b s) -> p b s", s=16)
        ueng = "gpsimd" if gq else "vector"
        if gq:
            wait("gpsimd", "sV", rt)
        vt = op(ueng, lambda e: e.tensor_tensor(out=uv, in0=srcv2, in1=rcb2,
                                                op=ALU.mult), inc="sG" if gq else "sV")
        wait("scalar", "sG" if gq else "sV", vt)
        op("scalar", lambda e: e.activation(out=qb[:, 0:N], in_=qb[:, 0:N],
                                            func=AF.Square))
        st = op("scalar", lambda e: e.sign(out=sgq[:, 0:N], in_=src[:, 0:N]), inc="sS")
        wait("vector", "sS", st)
        def cmpi(e, dst, i):
            if TW[i] == 1.0:
                return e.tensor_single_scalar(out=dst[:, 0:N], in_=qb[:, 0:N],
                                              scalar=T2[i], op=ALU.is_gt)
            return e.tensor_scalar(out=dst[:, 0:N], in0=qb[:, 0:N], scalar1=T2[i],
                                   scalar2=TW[i], op0=ALU.is_gt, op1=ALU.mult)
        if gq:
            op("vector", lambda e: cmpi(e, A, 0))
            op("vector", lambda e: cmpi(e, B, 1))
            op("vector", lambda e: cmpi(e, Cb, 2))
            op("vector", lambda e: cmpi(e, D, 3))
            v1 = op("vector", lambda e: cmpi(e, Eb, 4), inc="sV")
            wait("gpsimd", "sV", v1)
            op("gpsimd", lambda e: e.tensor_tensor(out=A[:, 0:N], in0=A[:, 0:N],
                                                   in1=B[:, 0:N], op=ALU.add))
            g1 = op("gpsimd", lambda e: e.tensor_tensor(out=Cb[:, 0:N], in0=Cb[:, 0:N],
                                                        in1=D[:, 0:N], op=ALU.add),
                    inc="sG")
            wait("vector", "sG", g1)
            op("vector", lambda e: cmpi(e, B, 5))
            v2 = op("vector", lambda e: cmpi(e, D, 6), inc="sV")
            wait("gpsimd", "sV", v2)
            op("gpsimd", lambda e: e.tensor_tensor(out=Eb[:, 0:N], in0=Eb[:, 0:N],
                                                   in1=B[:, 0:N], op=ALU.add))
            op("gpsimd", lambda e: e.tensor_tensor(out=A[:, 0:N], in0=A[:, 0:N],
                                                   in1=Cb[:, 0:N], op=ALU.add))
            op("gpsimd", lambda e: e.tensor_tensor(out=A[:, 0:N], in0=A[:, 0:N],
                                                   in1=Eb[:, 0:N], op=ALU.add))
            g2 = op("gpsimd", lambda e: e.tensor_tensor(out=A[:, 0:N], in0=A[:, 0:N],
                                                        in1=D[:, 0:N], op=ALU.add),
                    inc="sG")
            wait("vector", "sG", g2)
            v3 = op("vector", lambda e: e.tensor_tensor(out=B[:, 0:N], in0=A[:, 0:N],
                                                        in1=sgq[:, 0:N], op=ALU.mult),
                    inc="sV")
            wait("gpsimd", "sV", v3)
            shb = shq[:, 0:nb].rearrange("p (b o) -> p b o", o=1)
            fv = B[:, 0:N].rearrange("p (b s) -> p b s", s=16)
            shb2, fv2 = bass.broadcast_tensor_aps(shb, fv)
            qv = qb[:, 0:N].rearrange("p (b s) -> p b s", s=16)
            vt = op("gpsimd", lambda e: e.tensor_tensor(out=qv, in0=fv2, in1=shb2,
                                                        op=ALU.mult), inc="sG")
            last_s_guard[0] = ("sS", st)
            last_g_guard[0] = ("sG", vt)
            return ("sG", vt)
        op("vector", lambda e: cmpi(e, A, 0))
        op("vector", lambda e: cmpi(e, B, 1))
        op("vector", lambda e: cmpi(e, Cb, 2))
        op("vector", lambda e: cmpi(e, D, 3))
        op("vector", lambda e: cmpi(e, Eb, 4))
        dr("vector")
        op("vector", lambda e: e.tensor_tensor(out=A[:, 0:N], in0=A[:, 0:N],
                                               in1=B[:, 0:N], op=ALU.add))
        op("vector", lambda e: e.tensor_tensor(out=Cb[:, 0:N], in0=Cb[:, 0:N],
                                               in1=D[:, 0:N], op=ALU.add))
        dr("vector")
        op("vector", lambda e: cmpi(e, B, 5))
        op("vector", lambda e: cmpi(e, D, 6))
        op("vector", lambda e: e.tensor_tensor(out=A[:, 0:N], in0=A[:, 0:N],
                                               in1=Cb[:, 0:N], op=ALU.add))
        dr("vector")
        op("vector", lambda e: e.tensor_tensor(out=B[:, 0:N], in0=B[:, 0:N],
                                               in1=D[:, 0:N], op=ALU.add))
        dr("vector")
        op("vector", lambda e: e.tensor_tensor(out=A[:, 0:N], in0=A[:, 0:N],
                                               in1=B[:, 0:N], op=ALU.add))
        dr("vector")
        op("vector", lambda e: e.tensor_tensor(out=A[:, 0:N], in0=A[:, 0:N],
                                               in1=Eb[:, 0:N], op=ALU.add))
        dr("vector")
        op("vector", lambda e: e.tensor_tensor(out=B[:, 0:N], in0=A[:, 0:N],
                                               in1=sgq[:, 0:N], op=ALU.mult))
        dr("vector")
        shb = shq[:, 0:nb].rearrange("p (b o) -> p b o", o=1)
        fv = B[:, 0:N].rearrange("p (b s) -> p b s", s=16)
        shb2, fv2 = bass.broadcast_tensor_aps(shb, fv)
        qv = qb[:, 0:N].rearrange("p (b s) -> p b s", s=16)
        vt = op("vector", lambda e: e.tensor_tensor(out=qv, in0=fv2, in1=shb2,
                                                    op=ALU.mult), inc="sV")
        last_s_guard[0] = ("sS", st)
        return ("sV", vt)

    sp_guard = {}

    def split(src, N, hi_dst, lo_dst, pre, gname):
        waitt("scalar", pre)
        if gname in sp_guard:
            wait("scalar", "sP", sp_guard[gname])
        st = op("scalar", lambda e: e.copy(out=hi_dst[:, 0:N], in_=src[:, 0:N]), inc="sS")
        wait("vector", "sS", st)
        vt = op("vector", lambda e: e.scalar_tensor_tensor(
            out=lo_dst[:, 0:N], in0=hi_dst[:, 0:N], scalar=-1.0, in1=src[:, 0:N],
            op0=ALU.mult, op1=ALU.add), inc="sV")
        last_s_guard[0] = ("sS", st)
        return ("sS", st), ("sV", vt)

    trrot = [0]

    def transpose_bf16(src_ap, dst_ap, pre_v):
        nmi = trrot[0] % 2
        bank = (tr_c, tr_d)[nmi]
        bname = "trcd%d" % nmi
        trrot[0] += 1
        bank_wait("tensor", bname)
        waitt("tensor", pre_v)
        pt = op("tensor", lambda e, b=bank, s=src_ap: e.matmul(
            out=b[:], lhsT=s, rhs=idnb_s[:], is_transpose=True, start=True, stop=True),
            inc="sP")
        wait("scalar", "sP", pt)
        stok = op("scalar", lambda e, b=bank, d=dst_ap: e.copy(out=d, in_=b[:]), inc="sS")
        bank_set(bname, "sS", stok)
        return ("sS", stok), pt

    f32rot = [0]

    def transpose_f32(src_ap, pre_tok, rows=128):
        nmi = f32rot[0] % 2
        bank = (tr_a, tr_b)[nmi]
        bname = "trab%d" % nmi
        f32rot[0] += 1
        bank_wait("tensor", bname)
        waitt("tensor", pre_tok)
        pt = op("tensor", lambda e, b=bank, s=src_ap, r=rows: e.matmul(
            out=b[0:r, :], lhsT=s, rhs=idn_s[:], is_transpose=True, start=True, stop=True),
            inc="sP")
        return bank, pt, bname

    def tr_split(src_ap, hi_dst, lo_dst, pre):
        bk, ptk, bn = transpose_f32(src_ap, pre)
        wait("scalar", "sP", ptk)
        st = op("scalar", lambda e, b=bk, d=hi_dst: e.copy(out=d, in_=b[:]), inc="sS")
        wait("vector", "sS", st)
        vt = op("vector", lambda e, b=bk, h=hi_dst, l=lo_dst: e.scalar_tensor_tensor(
            out=l, in0=h, scalar=-1.0, in1=b[:], op0=ALU.mult, op1=ALU.add), inc="sV")
        bank_set(bn, "sV", vt)
        return st, vt, ptk

    ictx = ExitStack()
    ISB = lambda n, s, d=F32: ictx.enter_context(nc.sbuf_tensor(n, s, d))
    bel_s = ISB("bel_s", [1, 24])
    wle_f = ISB("wle_f", [24, E])
    wel_f = ISB("wel_f", [128, KCH * 24])
    fctx = ExitStack()
    FSB = lambda n, s, d=F32: fctx.enter_context(nc.sbuf_tensor(n, s, d))
    x0_s = FSB("x0_s", [1, E])
    xn0_s = FSB("xn0_s", [1, E])
    cnts_s = FSB("cnts_s", [1, 8])
    lx_s = FSB("lx_s", [1, 5])
    ly_s = FSB("ly_s", [1, 5])
    red_s = FSB("red_s", [1, 8])
    tmp5_s = FSB("tmp5_s", [1, 5])
    sc5_s = FSB("sc5_s", [1, 5])
    J_s = FSB("J_s", [128, 128])
    cosbuf = [[FSB("cosb_%d_%d" % (i, b2), [128, E]) for b2 in range(2)] for i in range(4)]
    c_sb = FSB("c_sb", [1, E])
    sh_buf = [FSB("sh_buf0", [128, E]), FSB("sh_buf1", [128, E])]
    cchi_s = FSB("cchi_s", [128, E], BF16)
    cclo_s = FSB("cclo_s", [128, E], BF16)

    t_id = dma("sync", "dCon", idn_s[:], dident)
    t_idb = dma("sync", "dCon", idnb_s[:], didentb)
    t_ht = dma("sync", "dCon", ht_s[:], dHt)
    t_kraw = dma("sync", "dCon", kraw_s[:], dkraw)
    t_sc5 = dma("sync", "dCon", sc5_s[:], dsc5)
    t_ones = dma("sync", "dCon", ones_s[:], dones)
    t_x0 = dma("sync", "dCon", x0_s[:], dx0)
    t_J = dma("sync", "dCon", J_s[:], dJ)
    t_wle = dma("sync", "dCon", wle_f[:], dwle)
    wel_src = bass.AP(tensor=dwel.tensor, offset=0, ap=[[24, 128], [128 * 24, KCH], [1, 24]])
    t_wel = dma("sync", "dCon", wel_f[:].rearrange("p (c j) -> p c j", j=24), wel_src)
    t_ble = dma("sync", "dCon", ble_s[:], dble)
    t_bel = dma("sync", "dCon", bel_s[:], dbel)
    dma("sync", "dCon", frD1_s[:], dfrD1)
    dma("sync", "dCon", fiD1_s[:], dfiD1)
    CONT = cnt["dCon"]
    t_id = t_idb = t_ht = t_kraw = t_sc5 = t_ones = t_x0 = t_J = CONT
    t_wle = t_wel = t_ble = t_bel = CONT

    wait("vector", "dCon", t_ones)
    for n, v in (("six", 6.0), ("five", 5.0), ("ten", 10.0), ("one", 1.0)):
        op("vector", lambda e, n=n, v=v: e.memset(sc1[n][:], v))
    dr("vector")
    op("vector", lambda e: e.reciprocal(out=sc1["r6"][:], in_=sc1["six"][:]))
    op("vector", lambda e: e.reciprocal(out=sc1["fifth"][:], in_=sc1["five"][:]))
    vt = op("vector", lambda e: e.reciprocal(out=sc1["tenth"][:], in_=sc1["ten"][:]), inc="sV")
    wait("scalar", "sV", vt)
    st = op("scalar", lambda e: e.activation(out=sc1["pi4"][:], in_=sc1["one"][:],
                                             func=AF.Arctan), inc="sS")
    wait("vector", "sS", st)
    op("vector", lambda e: e.tensor_single_scalar(out=sc1["pi"][:], in_=sc1["pi4"][:],
                                                  scalar=4.0, op=ALU.mult))
    dr("vector")
    op("vector", lambda e: e.tensor_single_scalar(out=sc1["pihalf"][:], in_=sc1["pi"][:],
                                                  scalar=0.5, op=ALU.mult))

    wait("scalar", "dCon", t_x0)
    st = op("scalar", lambda e: e.activation(out=scrB[0:1, :], in_=x0_s[:], func=AF.Square,
                                             accum_out=ssq_s[0:1, :]), inc="sS")
    wait("vector", "sS", st)
    wait("vector", "dCon", t_x0)
    op("vector", lambda e: e.reduce_sum(out=sum_s[0:1, :], in_=x0_s[:], axis=AX.X))
    dr("vector")
    op("vector", lambda e: e.tensor_single_scalar(out=m_s[0:1, :], in_=sum_s[0:1, :],
                                                  scalar=1.0 / 1024, op=ALU.mult))
    dr("vector")
    op("vector", lambda e: e.tensor_tensor(out=m2_s[0:1, :], in0=m_s[0:1, :],
                                           in1=m_s[0:1, :], op=ALU.mult))
    dr("vector")
    op("vector", lambda e: e.scalar_tensor_tensor(out=var_s[0:1, :], in0=ssq_s[0:1, :],
                                                  scalar=1.0 / 1024, in1=m2_s[0:1, :],
                                                  op0=ALU.mult, op1=ALU.subtract))
    dr("vector")
    vt = op("vector", lambda e: e.tensor_single_scalar(out=var_s[0:1, :], in_=var_s[0:1, :],
                                                       scalar=1e-5, op=ALU.add), inc="sV")
    wait("scalar", "sV", vt)
    st = op("scalar", lambda e: e.activation(out=sq_s[0:1, :], in_=var_s[0:1, :],
                                             func=AF.Sqrt), inc="sS")
    wait("vector", "sS", st)
    op("vector", lambda e: e.reciprocal(out=rstd_s[0:1, :], in_=sq_s[0:1, :]))
    dr("vector")
    vtdbg = op("vector", lambda e: e.tensor_scalar(out=xn0_s[:], in0=x0_s[:], scalar1=m_s[0:1, :],
                                           scalar2=rstd_s[0:1, :], op0=ALU.subtract,
                                           op1=ALU.mult), inc="sV")
    tap("x0row", x0_s[0:1, :], ("sV", vtdbg))
    tap("lnstats", ssq_s[0:1, :], ("sV", vtdbg))
    tap("lnsum", sum_s[0:1, :], ("sV", vtdbg))
    tap("lnvar", var_s[0:1, :], ("sV", vtdbg))
    tap("lnsq", sq_s[0:1, :], ("sV", vtdbg))
    tap("lnrstd", rstd_s[0:1, :], ("sV", vtdbg))
    tap("xn0_early", xn0_s[0:1, :], ("sV", vtdbg))
    for i, n2 in enumerate((4, 16, 64, 256, 1024)):
        dr("vector")
        op("vector", lambda e, n=n2: e.reduce_sum(out=m_s[0:1, :], in_=xn0_s[0:1, 0:n],
                                                  axis=AX.X))
        dr("vector")
        op("vector", lambda e, n=n2: e.tensor_single_scalar(out=m_s[0:1, :], in_=m_s[0:1, :],
                                                            scalar=1.0 / n, op=ALU.mult))
        dr("vector")
        op("vector", lambda e, n=n2: e.tensor_single_scalar(
            out=scrB[0:1, 0:n], in_=xn0_s[0:1, 0:n], scalar=m_s[0:1, :], op=ALU.is_gt))
        dr("vector")
        op("vector", lambda e, n=n2, i=i: e.reduce_sum(out=cnts_s[0:1, i:i + 1],
                                                       in_=scrB[0:1, 0:n], axis=AX.X))
        dr("vector")
        op("vector", lambda e, i=i: e.tensor_single_scalar(
            out=cnts_s[0:1, i:i + 1], in_=cnts_s[0:1, i:i + 1], scalar=1.0, op=ALU.max))
    dr("vector")
    vt = op("vector", lambda e: e.tensor_copy(out=red_s[0:1, 0:1], in_=cnts_s[0:1, 0:1]),
            inc="sV")
    wait("scalar", "sV", vt)
    wait("scalar", "dCon", t_sc5)
    op("scalar", lambda e: e.activation(out=lx_s[:], in_=sc5_s[:], func=AF.Ln))
    st = op("scalar", lambda e: e.activation(out=ly_s[:], in_=cnts_s[0:1, 0:5],
                                             func=AF.Ln), inc="sS")
    wait("vector", "sS", st)
    op("vector", lambda e: e.reduce_sum(out=red_s[0:1, 0:1], in_=lx_s[:], axis=AX.X))
    op("vector", lambda e: e.reduce_sum(out=red_s[0:1, 1:2], in_=ly_s[:], axis=AX.X))
    op("vector", lambda e: e.tensor_tensor(out=tmp5_s[:], in0=lx_s[:], in1=ly_s[:],
                                           op=ALU.mult))
    dr("vector")
    op("vector", lambda e: e.reduce_sum(out=red_s[0:1, 2:3], in_=tmp5_s[:], axis=AX.X))
    dr("vector")
    op("vector", lambda e: e.tensor_tensor(out=tmp5_s[:], in0=lx_s[:], in1=lx_s[:],
                                           op=ALU.mult))
    dr("vector")
    op("vector", lambda e: e.reduce_sum(out=red_s[0:1, 3:4], in_=tmp5_s[:], axis=AX.X))
    dr("vector")
    op("vector", lambda e: e.tensor_tensor(out=red_s[0:1, 4:5], in0=red_s[0:1, 0:1],
                                           in1=red_s[0:1, 1:2], op=ALU.mult))
    dr("vector")
    op("vector", lambda e: e.scalar_tensor_tensor(out=red_s[0:1, 4:5], in0=red_s[0:1, 2:3],
                                                  scalar=5.0, in1=red_s[0:1, 4:5],
                                                  op0=ALU.mult, op1=ALU.subtract))
    op("vector", lambda e: e.tensor_tensor(out=red_s[0:1, 5:6], in0=red_s[0:1, 0:1],
                                           in1=red_s[0:1, 0:1], op=ALU.mult))
    dr("vector")
    op("vector", lambda e: e.scalar_tensor_tensor(out=red_s[0:1, 5:6], in0=red_s[0:1, 3:4],
                                                  scalar=5.0, in1=red_s[0:1, 5:6],
                                                  op0=ALU.mult, op1=ALU.subtract))
    dr("vector")
    op("vector", lambda e: e.reciprocal(out=red_s[0:1, 6:7], in_=red_s[0:1, 5:6]))
    dr("vector")
    op("vector", lambda e: e.tensor_tensor(out=red_s[0:1, 7:8], in0=red_s[0:1, 4:5],
                                           in1=red_s[0:1, 6:7], op=ALU.mult))
    dr("vector")
    op("vector", lambda e: e.tensor_scalar(out=red_s[0:1, 7:8], in0=red_s[0:1, 7:8],
                                           scalar1=-1.0, scalar2=1.0, op0=ALU.mult,
                                           op1=ALU.max))
    dr("vector")
    vt = op("vector", lambda e: e.tensor_single_scalar(out=red_s[0:1, 7:8],
                                                       in_=red_s[0:1, 7:8], scalar=2.0,
                                                       op=ALU.min), inc="sV")
    tap("D", red_s[0:1, 0:8], ("sV", vt))
    tap("cnts", cnts_s[0:1, 0:5], ("sV", vt))
    tap("xn0", xn0_s[0:1, :], ("sV", vt))
    wait("tensor", "sV", vt)
    wait("tensor", "dCon", t_ones)
    pt = op("tensor", lambda e: e.matmul(out=mm_c[:, 0:1], lhsT=ones_s[:],
                                         rhs=red_s[0:1, 7:8], start=True, stop=True),
            inc="sP")
    wait("vector", "sP", pt)
    op("vector", lambda e: e.tensor_copy(out=sc1["Drep"][:], in_=mm_c[:, 0:1]))
    dr("vector")
    op("vector", lambda e: e.tensor_scalar(out=sc1["omega"][:], in0=sc1["Drep"][:],
                                           scalar1=sc1["fifth"][:], scalar2=1.0,
                                           op0=ALU.mult, op1=ALU.add))
    op("vector", lambda e: e.tensor_scalar(out=sc1["aco"][:], in0=sc1["Drep"][:],
                                           scalar1=0.5, scalar2=1.0, op0=ALU.mult,
                                           op1=ALU.add))
    dr("vector")
    op("vector", lambda e: e.tensor_single_scalar(out=sc1["nega"][:], in_=sc1["aco"][:],
                                                  scalar=-1.0, op=ALU.mult))
    op("vector", lambda e: e.tensor_tensor(out=sc1["bco"][:], in0=sc1["Drep"][:],
                                           in1=sc1["tenth"][:], op=ALU.mult))
    op("vector", lambda e: e.tensor_tensor(out=sc1["omD"][:], in0=sc1["omega"][:],
                                           in1=sc1["Drep"][:], op=ALU.mult))
    dr("vector")
    op("vector", lambda e: e.tensor_tensor(out=sc1["pmo"][:], in0=sc1["pi"][:],
                                           in1=sc1["omD"][:], op=ALU.subtract))
    dr("vector")
    bank_set("mm_c", "sV", op("vector", lambda e: e.tensor_copy(
        out=red_s[0:1, 1:2], in_=sc1["pmo"][0:1, :]), inc="sV"))
    wait("scalar", "dCon", t_kraw)
    st = op("scalar", lambda e: e.activation(out=ka_s[:], in_=kraw_s[:], func=AF.Abs),
            inc="sS")
    wait("vector", "sS", st)
    op("vector", lambda e: e.tensor_single_scalar(out=ka_s[:], in_=ka_s[:],
                                                  scalar=1e-8, op=ALU.add))
    dr("vector")
    op("vector", lambda e: e.tensor_tensor(out=ka2_s[:], in0=ka_s[:], in1=ka_s[:],
                                           op=ALU.mult))
    dr("vector")
    vt = op("vector", lambda e: e.tensor_scalar(out=real_s[:], in0=ka_s[:],
                                                scalar1=sc1["nega"][:],
                                                scalar2=sc1["pmo"][:], op0=ALU.mult,
                                                op1=ALU.add), inc="sV")
    wait("scalar", "sV", vt)
    st_r = op("scalar", lambda e: e.activation(out=real_s[:], in_=real_s[:], func=AF.Sin),
              inc="sS")
    op("vector", lambda e: e.tensor_scalar(out=ph_s[:], in0=ka2_s[:], scalar1=sc1["bco"][:],
                                           scalar2=sc1["omD"][:], op0=ALU.mult, op1=ALU.add))
    dr("vector")
    vt = op("vector", lambda e: e.tensor_tensor(out=ph_s[:], in0=ph_s[:], in1=ka_s[:],
                                                op=ALU.subtract), inc="sV")
    wait("scalar", "sV", vt)
    op("scalar", lambda e: e.activation(out=sph_s[:], in_=ph_s[:], func=AF.Sin))
    wait("vector", "sS", st_r)
    dr("vector")
    vt = op("vector", lambda e: e.tensor_scalar(out=cph_s[:], in0=ph_s[:], scalar1=-1.0,
                                                scalar2=sc1["pihalf"][:], op0=ALU.mult,
                                                op1=ALU.add), inc="sV")
    wait("scalar", "sV", vt)
    st = op("scalar", lambda e: e.activation(out=cph_s[:], in_=cph_s[:], func=AF.Sin),
            inc="sS")
    wait("vector", "sS", st)
    op("vector", lambda e: e.tensor_tensor(out=fr_s[:], in0=real_s[:], in1=cph_s[:],
                                           op=ALU.mult))
    op("vector", lambda e: e.tensor_tensor(out=fi_s[:], in0=real_s[:], in1=sph_s[:],
                                           op=ALU.mult))
    dr("vector")
    op("vector", lambda e: e.tensor_single_scalar(out=sc1["sel"][:], in_=sc1["Drep"][:],
                                                  scalar=1.0, op=ALU.is_equal))
    dr("vector")
    op("vector", lambda e: e.tensor_tensor(out=real_s[:], in0=frD1_s[:], in1=fr_s[:],
                                           op=ALU.subtract))
    op("vector", lambda e: e.tensor_tensor(out=ph_s[:], in0=fiD1_s[:], in1=fi_s[:],
                                           op=ALU.subtract))
    dr("vector")
    op("vector", lambda e: e.scalar_tensor_tensor(out=fr_s[:], in0=real_s[:],
                                                  scalar=sc1["sel"][:], in1=fr_s[:],
                                                  op0=ALU.mult, op1=ALU.add))
    op("vector", lambda e: e.scalar_tensor_tensor(out=fi_s[:], in0=ph_s[:],
                                                  scalar=sc1["sel"][:], in1=fi_s[:],
                                                  op0=ALU.mult, op1=ALU.add))
    dr("vector")
    vt_fr = op("vector", lambda e: e.tensor_single_scalar(out=nfi_s[:], in_=fi_s[:],
                                                          scalar=-1.0, op=ALU.mult),
               inc="sV")
    tap("fr", fr_s[:], ("sV", vt_fr))
    tap("fi", fi_s[:], ("sV", vt_fr))

    cchunk_done = {}
    bank_wait("tensor", "mm_c")
    for c in range(KCH):
        b = c % 2
        toks = []
        for a in range(4):
            if c >= 2:
                wait("sync", "sP", cchunk_done[c - 2])
            toks.append(dma("sync", "dCos%d" % b, cosbuf[a][b][:],
                            dcos[a][c * 128:(c + 1) * 128, :]))
        wait("tensor", "dCos%d" % b, toks[3])
        if c == 0:
            wait("tensor", "sV", vt_fr)
        frc = fr_s[:, c:c + 1]
        nfc = nfi_s[:, c:c + 1]
        tk = None
        for half, bank in ((0, mm_c), (1, mm_e)):
            sl = slice(half * 512, half * 512 + 512)
            op("tensor", lambda e, b2=bank, f=frc, s=sl, b3=b, sf=(c == 0): e.matmul(
                out=b2[0:1, :], lhsT=f, rhs=cosbuf[0][b3][:, s], start=sf, stop=False))
            op("tensor", lambda e, b2=bank, f=frc, s=sl, b3=b: e.matmul(
                out=b2[0:1, :], lhsT=f, rhs=cosbuf[1][b3][:, s], start=False, stop=False))
            op("tensor", lambda e, b2=bank, f=nfc, s=sl, b3=b: e.matmul(
                out=b2[0:1, :], lhsT=f, rhs=cosbuf[2][b3][:, s], start=False, stop=False))
            tk = op("tensor", lambda e, b2=bank, f=nfc, s=sl, b3=b, sp=(c == KCH - 1):
                    e.matmul(out=b2[0:1, :], lhsT=f, rhs=cosbuf[3][b3][:, s], start=False,
                             stop=sp), inc="sP")
        cchunk_done[c] = tk
    wait("vector", "sP", cchunk_done[KCH - 1])
    op("vector", lambda e: e.tensor_single_scalar(out=c_sb[0:1, 0:512], in_=mm_c[0:1, :],
                                                  scalar=1.0 / 1024, op=ALU.mult))
    vt = op("vector", lambda e: e.tensor_single_scalar(out=c_sb[0:1, 512:1024],
                                                       in_=mm_e[0:1, :],
                                                       scalar=1.0 / 1024, op=ALU.mult),
            inc="sV")
    bank_set("mm_c", "sV", vt)
    bank_set("mm_e", "sV", vt)
    tap("cvec", c_sb[0:1, :], ("sV", vt))
    wait("scalar", "sV", vt)
    dma("scalar", "dC2", dc2[0:1, 0:1024], c_sb[:])
    tc2 = dma("scalar", "dC2", dc2[0:1, 1024:2048], c_sb[:])
    shdone = {}
    for c in range(KCH):
        M0 = c * 128
        b = c % 2
        wait("gpsimd", "dC2", tc2)
        if c >= 2:
            wait("gpsimd", "sP", shdone[c - 2])
        src = bass.AP(tensor=dc2.tensor, offset=1024 - 127 - M0, ap=[[1, 128], [1, 1024]])
        tsh = dma("gpsimd", "dSh%d" % b, sh_buf[b][:], src)
        wait("tensor", "dSh%d" % b, tsh)
        if c == 0:
            wait("tensor", "dCon", t_J)
        bank_wait("tensor", "mm_a")
        bank_wait("tensor", "mm_b")
        op("tensor", lambda e, b2=b: e.matmul(out=mm_a[:], lhsT=J_s[:],
                                              rhs=sh_buf[b2][:, 0:512], start=True,
                                              stop=True))
        tk = op("tensor", lambda e, b2=b: e.matmul(out=mm_b[:], lhsT=J_s[:],
                                                   rhs=sh_buf[b2][:, 512:1024],
                                                   start=True, stop=True), inc="sP")
        shdone[c] = tk
        wait("scalar", "sP", tk)
        if c >= 1:
            wait("scalar", "dA", cstore)
        op("scalar", lambda e: e.copy(out=cchi_s[:, 0:512], in_=mm_a[:]))
        st = op("scalar", lambda e: e.copy(out=cchi_s[:, 512:1024], in_=mm_b[:]), inc="sS")
        wait("vector", "sS", st)
        op("vector", lambda e: e.scalar_tensor_tensor(out=cclo_s[:, 0:512],
                                                      in0=cchi_s[:, 0:512], scalar=-1.0,
                                                      in1=mm_a[:], op0=ALU.mult,
                                                      op1=ALU.add))
        vt = op("vector", lambda e: e.scalar_tensor_tensor(out=cclo_s[:, 512:1024],
                                                           in0=cchi_s[:, 512:1024],
                                                           scalar=-1.0, in1=mm_b[:],
                                                           op0=ALU.mult, op1=ALU.add),
                inc="sV")
        bank_set("mm_a", "sV", vt)
        bank_set("mm_b", "sV", vt)
        if c == 0:
            tap("Chi0", cchi_s[:], ("sV", vt))
        wait("scalar", "sV", vt)
        dma("scalar", "dA", dChi[M0:M0 + 128, :], cchi_s[:])
        cstore = dma("scalar", "dA", dClo[M0:M0 + 128, :], cclo_s[:])

    fctx.close()
    NW = 4096
    wnat4 = [ISB("wnat4_0", [128, NW]), ISB("wnat4_1", [128, NW])]
    q4_s = ISB("q4_s", [128, NW])
    A4 = ISB("A4", [128, NW], BF16)
    B4 = ISB("B4", [128, NW], BF16)
    C4 = ISB("C4", [128, NW], BF16)
    D4 = ISB("D4", [128, NW], BF16)
    E4 = ISB("E4", [128, NW], BF16)
    sgn4 = ISB("sgn4", [128, NW], BF16)
    sp4 = [ISB("sp4h", [128, NW], BF16), ISB("sp4l", [128, NW], BF16)]
    tstg4 = [ISB("tstg4_0", [128, NW], BF16), ISB("tstg4_1", [128, NW], BF16)]
    bm4_s = ISB("bm4_s", [128, 256])
    sc4_s = ISB("sc4_s", [128, 256])
    sh4_s = ISB("sh4_s", [128, 256])
    rc4_s = ISB("rc4_s", [128, 256])
    QB = (q4_s, bm4_s, sc4_s, sh4_s, rc4_s, A4, B4, C4, D4, E4, sgn4)
    quads = []
    for wname, dsrc, ddst, nchunks, width in (("g", dwg, dwgt, ICH, E),
                                              ("u", dwu, dwut, ICH, E),
                                              ("d", dwd, dwdt, KCH, I)):
        if width == E:
            ch = 0
            while ch < nchunks:
                nch = min(4, nchunks - ch)
                quads.append((dsrc, ddst, ch, nch, E, False))
                ch += nch
        else:
            for ch in range(nchunks):
                quads.append((dsrc, ddst, ch, 1, I, True))
    pidx = 0
    wnat_free = {}
    wstore = None
    waitt("sync", ("dA", cstore))
    for dsrc, ddst, ch0, nch, clen, isd in quads:
        b = pidx % 2
        pidx += 1
        plen = nch * clen
        for g in wnat_free.get(b, ()):
            waitt("sync", g)
        if isd:
            wt = dma("sync", "dWn%d" % b, wnat4[b][:, 0:plen],
                     dsrc[ch0 * 128:(ch0 + 1) * 128, 0:plen])
        else:
            apw = bass.AP(tensor=dsrc.tensor, offset=ch0 * 128 * E,
                          ap=[[E, 128], [128 * E, nch], [1, E]])
            wt = dma("sync", "dWn%d" % b,
                     wnat4[b][:, 0:plen].rearrange("p (c n) -> p c n", n=E), apw)
        wait("vector", "dWn%d" % b, wt)
        wait("scalar", "dWn%d" % b, wt)
        qt = quant(wnat4[b], plen, bufs=QB)
        wnat_free[b] = (qt, last_s_guard[0])
        s_tok, v_tok = split(q4_s, plen, sp4[0], sp4[1], qt, "wp")
        nblk = plen // 128
        lastev = None
        for half, stg, vguard in ((0, sp4[0], ("sS", s_tok[1])),
                                  (1, sp4[1], v_tok)):
            tgt = tstg4[half]
            for blk in range(nblk):
                if blk == 0 and wstore is not None:
                    wait("scalar", "dA", wstore)
                lastev, _ = transpose_bf16(stg[:, blk * 128:(blk + 1) * 128],
                                           tgt[:, blk * 128:(blk + 1) * 128],
                                           vguard)
        sp_guard["wp"] = cnt["sP"]
        wait("scalar", "sS", lastev[1])
        for half in range(2):
            tgt = tstg4[half]
            dd = ddst[half]
            if isd:
                dstap = bass.AP(tensor=dd.tensor, offset=ch0 * 128,
                                ap=[[E, 128], [128 * E, nblk], [1, 128]])
                wstore = dma("scalar", "dA", dstap,
                             tgt[:, 0:nblk * 128].rearrange("p (c n) -> p c n", n=128))
            else:
                for c in range(nch):
                    dstap = bass.AP(tensor=dd.tensor, offset=(ch0 + c) * 128,
                                    ap=[[I, 128], [128 * I, 8], [1, 128]])
                    wstore = dma("scalar", "dA", dstap,
                                 tgt[:, c * E:(c + 1) * E].rearrange(
                                     "p (c2 n) -> p c2 n", n=128))

    wait("scalar", "dCon", t_wel)
    split(wel_f, KCH * 24, wel_b[0], wel_b[1], None, "wel")
    wait("scalar", "dCon", t_wle)
    st = op("scalar", lambda e: e.copy(out=wle_b[0][:], in_=wle_f[:]), inc="sS")
    wait("vector", "sS", st)
    op("vector", lambda e: e.scalar_tensor_tensor(out=wle_b[1][:], in0=wle_b[0][:],
                                                  scalar=-1.0, in1=wle_f[:],
                                                  op0=ALU.mult, op1=ALU.add))
    wait("tensor", "dCon", t_bel)
    bank_wait("tensor", "mm_c")
    pt = op("tensor", lambda e: e.matmul(out=mm_c[:, 0:24], lhsT=ones_s[:], rhs=bel_s[:],
                                         start=True, stop=True), inc="sP")
    wait("vector", "sP", pt)
    vt = op("vector", lambda e: e.tensor_single_scalar(out=negb_s[:], in_=mm_c[:, 0:24],
                                                       scalar=-1.0, op=ALU.mult), inc="sV")
    bank_set("mm_c", "sV", vt)

    init_join = dict((s, cnt[s]) for s in semnames)
    ictx.close()

    arenaH = SB("arenaH", [128, 22528], BF16)
    arenaX = SB("arenaX", [128, 16384], BF16)
    sT = SB("sT", [128, KCH * 512], F32)
    xstage = [SB("xstage0", [128, E]), SB("xstage1", [128, E])]
    wsl = {w: [[SB("wsl_%s_%d_%d" % (w, h, b2), [128, KCH * NSL], BF16)
                for b2 in range(2)] for h in range(2)] for w in ("g", "u")}
    wdb = [[SB("wd_%d_%d" % (h, b2), [128, 3 * 512], BF16) for b2 in range(2)]
           for h in range(2)]
    cstb = [[SB("cst_%d_%d" % (h, b2), [128, E], BF16) for b2 in range(2)] for h in range(2)]
    gsc = SB("gsc", [128, 512])
    gdbg = SB("gdbg", [128, 512]) if DEBUG else None
    outst = SB("outst", [128, NT * 512])
    xre = SB("xre", [128, NT * 512])

    def xnT(h):
        return arenaX[:, h * 4096:(h + 1) * 4096]

    def xqT(h):
        return arenaX[:, 8192 + h * 4096: 8192 + (h + 1) * 4096]

    def x1T(h):
        return arenaH[:, h * 4096:(h + 1) * 4096]

    def hqT(h, c):
        return arenaH[:, h * 11264 + c * 512: h * 11264 + (c + 1) * 512]

    for eng in ENGS:
        for s, v in init_join.items():
            wait(eng, s, v)

    G = dict(xst_free=[None, None], wsl_free=[None, None], wd_free=[None, None],
             cst_free=[None, None], xre_free=None, outst_free=None, ssq_free=None,
             scrC_free=None, gsc_free=None, arena_ready=(0, 0), hq_ready=(0, 0))

    def ph1_front(stp, tt):
        base = stp * TSUP
        r0 = base + tt * 128
        b = (stp * NT + tt) % 2
        waitt("sync", G["xst_free"][b])
        xt = dma("sync", "dX%d" % b, xstage[b][:], dx[r0:r0 + 128, :])
        wait("scalar", "dX%d" % b, xt)
        waitt("scalar", G["ssq_free"])
        waitt("scalar", G["scrC_free"])
        waitt("vector", G["scrC_free"])
        st = op("scalar", lambda e, b=b: e.activation(out=scrB[:], in_=xstage[b][:],
                                                      func=AF.Square,
                                                      accum_out=ssq_s[:]), inc="sS")
        wait("vector", "dX%d" % b, xt)
        wait("vector", "sS", st)
        op("vector", lambda e, b=b: e.reduce_sum(out=sum_s[:], in_=xstage[b][:], axis=AX.X))
        dr("vector")
        op("vector", lambda e: e.tensor_single_scalar(out=m_s[:], in_=sum_s[:],
                                                      scalar=1.0 / 1024, op=ALU.mult))
        dr("vector")
        op("vector", lambda e: e.tensor_tensor(out=m2_s[:], in0=m_s[:], in1=m_s[:],
                                               op=ALU.mult))
        dr("vector")
        op("vector", lambda e: e.scalar_tensor_tensor(out=var_s[:], in0=ssq_s[:],
                                                      scalar=1.0 / 1024, in1=m2_s[:],
                                                      op0=ALU.mult, op1=ALU.subtract))
        dr("vector")
        vt = op("vector", lambda e: e.tensor_single_scalar(out=var_s[:], in_=var_s[:],
                                                           scalar=1e-5, op=ALU.add),
                inc="sV")
        G["ssq_free"] = ("sV", vt)
        wait("scalar", "sV", vt)
        st = op("scalar", lambda e: e.activation(out=sq_s[:], in_=var_s[:], func=AF.Sqrt),
                inc="sS")
        wait("vector", "sS", st)
        op("vector", lambda e: e.reciprocal(out=rstd_s[:], in_=sq_s[:]))
        dr("vector")
        vt = op("vector", lambda e, b=b: e.tensor_scalar(
            out=scrA[:], in0=xstage[b][:], scalar1=m_s[:], scalar2=rstd_s[:],
            op0=ALU.subtract, op1=ALU.mult), inc="sV")
        G["xst_free"][b] = ("sV", vt)
        s_xn, v_xn = split(scrA, E, sp_hi, sp_lo, ("sV", vt), "p1a")
        qt = quant(scrA, E)
        G["scrC_free"] = qt
        return (s_xn, v_xn, qt)

    def ph1_tr(stp, tt, toks):
        s_xn, v_xn, qt = toks
        for srcbuf, dst, vg in ((sp_hi, xnT(0), ("sS", s_xn[1])),
                                (sp_lo, xnT(1), v_xn)):
            for c in range(KCH):
                dcol = c * 512 + tt * 128
                transpose_bf16(srcbuf[:, c * 128:(c + 1) * 128],
                               dst[:, dcol:dcol + 128], vg)
        sp_guard["p1a"] = cnt["sP"]
        ptk = None
        for c in range(KCH):
            dcol = c * 512 + tt * 128
            st, vt2, ptk = tr_split(scrB[:, c * 128:(c + 1) * 128],
                                    xqT(0)[:, dcol:dcol + 128],
                                    xqT(1)[:, dcol:dcol + 128],
                                    qt if c == 0 else None)
        G["scrC_free"] = ("sP", ptk)

    def ph2(stp):
        for c in range(KCH):
            bname = "mm_a" if c % 2 == 0 else "mm_b"
            bnk = MMB[bname]
            cb = c % 2
            waitt("sync", G["cst_free"][cb])
            aph = bass.AP(tensor=dChi.tensor, offset=c * 128,
                          ap=[[E, 128], [128 * E, KCH], [1, 128]])
            apl = bass.AP(tensor=dClo.tensor, offset=c * 128,
                          ap=[[E, 128], [128 * E, KCH], [1, 128]])
            dv = cstb[0][cb][:].rearrange("p (m n) -> p m n", n=128)
            dv2 = cstb[1][cb][:].rearrange("p (m n) -> p m n", n=128)
            th = dma("sync", "dCc%d" % cb, dv, aph)
            tl = dma("sync", "dCc%d" % cb, dv2, apl)
            bank_wait("tensor", bname)
            wait("tensor", "dCc%d" % cb, tl)
            if c == 0:
                wait("tensor", "sS", G["arena_ready"][0])
                wait("tensor", "sV", G["arena_ready"][1])
            tk = None
            for m in range(KCH):
                chi = cstb[0][cb][:, m * 128:(m + 1) * 128]
                clo = cstb[1][cb][:, m * 128:(m + 1) * 128]
                xh = xnT(0)[:, m * 512:(m + 1) * 512]
                xl = xnT(1)[:, m * 512:(m + 1) * 512]
                op("tensor", lambda e, a=chi, r=xh, b2=bnk, sf=(m == 0): e.matmul(
                    out=b2[:], lhsT=a, rhs=r, start=sf, stop=False))
                op("tensor", lambda e, a=chi, r=xl, b2=bnk: e.matmul(
                    out=b2[:], lhsT=a, rhs=r, start=False, stop=False))
                tk = op("tensor", lambda e, a=clo, r=xh, b2=bnk, sp=(m == KCH - 1):
                        e.matmul(out=b2[:], lhsT=a, rhs=r, start=False, stop=sp), inc="sP")
            G["cst_free"][cb] = ("sP", tk)
            wait("scalar", "sP", tk)
            st = op("scalar", lambda e, b2=bnk, c=c: e.copy(
                out=x1T(0)[:, c * 512:(c + 1) * 512], in_=b2[:]), inc="sS")
            wait("vector", "sS", st)
            op("vector", lambda e, b2=bnk, c=c: e.scalar_tensor_tensor(
                out=x1T(1)[:, c * 512:(c + 1) * 512], in0=x1T(0)[:, c * 512:(c + 1) * 512],
                scalar=-1.0, in1=b2[:], op0=ALU.mult, op1=ALU.add))
            vt = op("vector", lambda e, b2=bnk, c=c: e.tensor_copy(
                out=sT[:, c * 512:(c + 1) * 512], in_=b2[:]), inc="sV")
            bank_set(bname, "sV", vt)
        x1_ready_s = cnt["sS"]
        x1_ready_v = cnt["sV"]
        for ts4 in range(NT):
            bank_wait("tensor", "mm_c")
            wait("tensor", "sS", x1_ready_s)
            wait("tensor", "sV", x1_ready_v)
            tk = None
            for c in range(KCH):
                xh = x1T(0)[:, c * 512 + ts4 * 128: c * 512 + ts4 * 128 + 128]
                xl = x1T(1)[:, c * 512 + ts4 * 128: c * 512 + ts4 * 128 + 128]
                wh = wel_b[0][:, c * 24:(c + 1) * 24]
                wl = wel_b[1][:, c * 24:(c + 1) * 24]
                op("tensor", lambda e, a=xh, r=wh, sf=(c == 0): e.matmul(
                    out=mm_c[:, 0:24], lhsT=a, rhs=r, start=sf, stop=False))
                op("tensor", lambda e, a=xh, r=wl: e.matmul(
                    out=mm_c[:, 0:24], lhsT=a, rhs=r, start=False, stop=False))
                tk = op("tensor", lambda e, a=xl, r=wh, sp=(c == KCH - 1): e.matmul(
                    out=mm_c[:, 0:24], lhsT=a, rhs=r, start=False, stop=sp), inc="sP")
            wait("vector", "sP", tk)
            vt = op("vector", lambda e: e.tensor_tensor(out=bits_s[:], in0=mm_c[:, 0:24],
                                                        in1=negb_s[:], op=ALU.is_gt),
                    inc="sV")
            bank_set("mm_c", "sV", vt)
            bk, ptk, bn = transpose_f32(bits_s[:], ("sV", vt), rows=24)
            wait("scalar", "sP", ptk)
            st = op("scalar", lambda e, b2=bk: e.copy(out=bitsT_s[:], in_=b2[0:24, :]),
                    inc="sS")
            bank_set(bn, "sS", st)
            bank_wait("tensor", "mm_e")
            wait("tensor", "sS", st)
            ptk = op("tensor", lambda e: e.matmul(out=mm_e[:, 0:12], lhsT=bitsT_s[:],
                                                  rhs=ht_s[:], start=True, stop=True),
                     inc="sP")
            wait("vector", "sP", ptk)
            vt = op("vector", lambda e: e.tensor_single_scalar(out=syn_s[:],
                                                               in_=mm_e[:, 0:12],
                                                               scalar=0.5, op=ALU.is_gt),
                    inc="sV")
            bank_set("mm_e", "sV", vt)
            dr("vector")
            op("vector", lambda e: e.reduce_sum(out=wgol_s[:], in_=syn_s[:], axis=AX.X))
            dr("vector")
            op("vector", lambda e: e.tensor_single_scalar(out=fixm_s[:], in_=wgol_s[:],
                                                          scalar=0.5, op=ALU.is_gt))
            op("vector", lambda e: e.tensor_single_scalar(out=c2_s[:], in_=wgol_s[:],
                                                          scalar=3.5, op=ALU.is_gt))
            dr("vector")
            op("vector", lambda e: e.tensor_tensor(out=fixm_s[:], in0=fixm_s[:],
                                                   in1=c2_s[:], op=ALU.subtract))
            dr("vector")
            op("vector", lambda e: e.tensor_single_scalar(out=p12_s[:], in_=syn_s[:],
                                                          scalar=fixm_s[:], op=ALU.mult))
            dr("vector")
            op("vector", lambda e: e.tensor_tensor(out=r12_s[:], in0=bits_s[:, 0:12],
                                                   in1=p12_s[:], op=ALU.mult))
            op("vector", lambda e: e.tensor_tensor(out=t12_s[:], in0=bits_s[:, 0:12],
                                                   in1=p12_s[:], op=ALU.add))
            dr("vector")
            op("vector", lambda e: e.scalar_tensor_tensor(out=corr_s[:, 0:12], in0=r12_s[:],
                                                          scalar=-2.0, in1=t12_s[:],
                                                          op0=ALU.mult, op1=ALU.add))
            dr("vector")
            vt = op("vector", lambda e: e.tensor_copy(out=corr_s[:, 12:24],
                                                      in_=bits_s[:, 12:24]), inc="sV")
            bk, ptk, bn = transpose_f32(corr_s[:], ("sV", vt), rows=24)
            wait("scalar", "sP", ptk)
            st = op("scalar", lambda e, b2=bk, t4=ts4: e.copy(
                out=corrT_s[:, t4 * 128:(t4 + 1) * 128], in_=b2[0:24, :]), inc="sS")
            bank_set(bn, "sS", st)
        corrT_ready = cnt["sS"]
        for ec in range(KCH):
            bank_wait("tensor", "mm_e")
            wait("tensor", "sS", corrT_ready)
            op("tensor", lambda e, ec=ec: e.matmul(
                out=mm_e[:], lhsT=wle_b[0][:, ec * 128:(ec + 1) * 128], rhs=corrT_s[:],
                start=True, stop=False))
            tk = op("tensor", lambda e, ec=ec: e.matmul(
                out=mm_e[:], lhsT=wle_b[1][:, ec * 128:(ec + 1) * 128], rhs=corrT_s[:],
                start=False, stop=True), inc="sP")
            wait("vector", "sP", tk)
            vt = op("vector", lambda e, ec=ec: e.scalar_tensor_tensor(
                out=sT[:, ec * 512:(ec + 1) * 512], in0=mm_e[:],
                scalar=ble_s[:, ec:ec + 1], in1=sT[:, ec * 512:(ec + 1) * 512],
                op0=ALU.add, op1=ALU.add), inc="sV")
            bank_set("mm_e", "sV", vt)

    def ph3_quantblock(ns, h4):
        waitt("vector", G["scrC_free"])
        waitt("scalar", G["scrC_free"])
        qt = quant(h4, E)
        ptk = None
        for cidx in range(KCH):
            tt = cidx // 2
            chunk = ns * 2 + (cidx % 2)
            tcol = tt * 128
            st, vt2, ptk = tr_split(scrB[:, cidx * 128:(cidx + 1) * 128],
                                    hqT(0, chunk)[:, tcol:tcol + 128],
                                    hqT(1, chunk)[:, tcol:tcol + 128],
                                    qt if cidx == 0 else None)
        G["scrC_free"] = ("sP", ptk)
        return qt

    def ph3(stp):
        h4bufs = [scrA, xstage[0]]
        for ns in range(NSLC):
            sb2 = ns % 2
            h4 = h4bufs[sb2]
            waitt("sync", G["wsl_free"][sb2])
            wt = None
            for w in ("g", "u"):
                dd = dwgt if w == "g" else dwut
                for h in range(2):
                    apw = bass.AP(tensor=dd[h].tensor, offset=ns * NSL,
                                  ap=[[I, 128], [128 * I, KCH], [1, NSL]])
                    wt = dma("sync", "dWs%d" % sb2, wsl[w][h][sb2][:].rearrange(
                        "p (c n) -> p c n", n=NSL), apw)
            for tt in range(NT):
                gbank, ubank = (("mm_a", "mm_b") if tt % 2 == 0 else ("mm_c", "mm_e"))
                gb, ub = MMB[gbank], MMB[ubank]
                bank_wait("tensor", gbank)
                bank_wait("tensor", ubank)
                wait("tensor", "dWs%d" % sb2, wt)
                if ns == 0 and tt == 0:
                    wait("tensor", "sS", G["arena_ready"][0])
                wait("tensor", "sV", G["arena_ready"][1])
                tk = None
                for m in range(KCH):
                    xh = xqT(0)[:, m * 512 + tt * 128: m * 512 + tt * 128 + 128]
                    xl = xqT(1)[:, m * 512 + tt * 128: m * 512 + tt * 128 + 128]
                    gh = wsl["g"][0][sb2][:, m * NSL:(m + 1) * NSL]
                    gl = wsl["g"][1][sb2][:, m * NSL:(m + 1) * NSL]
                    uh = wsl["u"][0][sb2][:, m * NSL:(m + 1) * NSL]
                    ul = wsl["u"][1][sb2][:, m * NSL:(m + 1) * NSL]
                    sf = (m == 0)
                    sp = (m == KCH - 1)
                    op("tensor", lambda e, a=xh, r=gh, b2=gb, sf=sf: e.matmul(
                        out=b2[:, 0:NSL], lhsT=a, rhs=r, start=sf, stop=False))
                    op("tensor", lambda e, a=xh, r=gl, b2=gb: e.matmul(
                        out=b2[:, 0:NSL], lhsT=a, rhs=r, start=False, stop=False))
                    op("tensor", lambda e, a=xh, r=uh, b2=ub, sf=sf: e.matmul(
                        out=b2[:, 0:NSL], lhsT=a, rhs=r, start=sf, stop=False))
                    op("tensor", lambda e, a=xh, r=ul, b2=ub: e.matmul(
                        out=b2[:, 0:NSL], lhsT=a, rhs=r, start=False, stop=False))
                    op("tensor", lambda e, a=xl, r=gh, b2=gb, sp=sp: e.matmul(
                        out=b2[:, 0:NSL], lhsT=a, rhs=r, start=False, stop=sp))
                    tk = op("tensor", lambda e, a=xl, r=uh, b2=ub, sp=sp: e.matmul(
                        out=b2[:, 0:NSL], lhsT=a, rhs=r, start=False, stop=sp), inc="sP")
                G["wsl_free"][sb2] = ("sP", tk)
                wait("scalar", "sP", tk)
                waitt("scalar", G["gsc_free"])
                st = op("scalar", lambda e, b2=gb: e.activation(
                    out=gsc[:, 0:NSL], in_=b2[:, 0:NSL], func=AF.Silu), inc="sS")
                wait("vector", "sS", st)
                vt = op("vector", lambda e, b2=ub, tt=tt, h4=h4: e.tensor_tensor(
                    out=h4[:, tt * NSL:(tt + 1) * NSL], in0=gsc[:, 0:NSL],
                    in1=b2[:, 0:NSL], op=ALU.mult), inc="sV")
                G["gsc_free"] = ("sV", vt)
                bank_set(gbank, "sS", st)
                bank_set(ubank, "sV", vt)
                last_s_guard[0] = ("sS", st)
            if ns >= 1:
                ph3_quantblock(ns - 1, h4bufs[(ns - 1) % 2])
        qt = ph3_quantblock(NSLC - 1, h4bufs[(NSLC - 1) % 2])
        G["hq_ready"] = (cnt["sS"], cnt["sV"])
        G["xst_free"][0] = qt

    def ph4_mm(stp, eh):
        base = stp * TSUP
        waitt("gpsimd", G["xre_free"])
        xt_last = None
        for tt in range(NT):
            r0 = base + tt * 128
            xt_last = dma("gpsimd", "dXre", xre[:, tt * 512:(tt + 1) * 512],
                          dx[r0:r0 + 128, eh * 512:(eh + 1) * 512])
        ebanks = ["mm_a", "mm_b", "mm_c", "mm_e"]
        for bn in ebanks:
            bank_wait("tensor", bn)
        wait("tensor", "sS", G["hq_ready"][0])
        wait("tensor", "sV", G["hq_ready"][1])
        ngrp = (ICH + 2) // 3
        for grp in range(ngrp):
            wb = grp % 2
            nch = min(3, ICH - grp * 3)
            waitt("sync", G["wd_free"][wb])
            t2 = None
            for h in range(2):
                apw = bass.AP(tensor=dwdt[h].tensor,
                              offset=grp * 3 * 128 * E + eh * 512,
                              ap=[[E, 128], [128 * E, nch], [1, 512]])
                t2 = dma("sync", "dWd%d" % wb,
                         wdb[h][wb][:, 0:nch * 512].rearrange("p (c n) -> p c n", n=512),
                         apw)
            wait("tensor", "dWd%d" % wb, t2)
            tk = None
            for sub in range(nch):
                ncn = grp * 3 + sub
                for ecl in range(4):
                    b2 = MMB[ebanks[ecl]]
                    wh = wdb[0][wb][:, sub * 512 + ecl * 128: sub * 512 + (ecl + 1) * 128]
                    wl = wdb[1][wb][:, sub * 512 + ecl * 128: sub * 512 + (ecl + 1) * 128]
                    hh = hqT(0, ncn)
                    hl = hqT(1, ncn)
                    sf = (ncn == 0)
                    sp = (ncn == ICH - 1)
                    op("tensor", lambda e, a=wh, r=hh, b3=b2, sf=sf: e.matmul(
                        out=b3[:], lhsT=a, rhs=r, start=sf, stop=False))
                    op("tensor", lambda e, a=wh, r=hl, b3=b2: e.matmul(
                        out=b3[:], lhsT=a, rhs=r, start=False, stop=False))
                    tk = op("tensor", lambda e, a=wl, r=hh, b3=b2, sp=sp: e.matmul(
                        out=b3[:], lhsT=a, rhs=r, start=False, stop=sp), inc="sP")
            G["wd_free"][wb] = ("sP", tk)
        return xt_last, cnt["sP"]

    def ph4_evac(stp, eh, xt_last, mmtok):
        base = stp * TSUP
        ebanks = ["mm_a", "mm_b", "mm_c", "mm_e"]
        for ecl in range(4):
            ec = eh * 4 + ecl
            b2 = MMB[ebanks[ecl]]
            wait("scalar", "sP", mmtok)
            waitt("scalar", G["gsc_free"])
            st = op("scalar", lambda e, b3=b2: e.activation(out=gsc[:], in_=b3[:],
                                                            func=AF.Silu), inc="sS")
            if DEBUG and stp == 0 and eh == 0 and ecl == 0:
                st = op("scalar", lambda e, b3=b2: e.copy(out=gdbg[:], in_=b3[:]),
                        inc="sS")
                tap("d0raw", gdbg[:], ("sS", st))
            wait("vector", "sS", st)
            waitt("vector", G["outst_free"])
            vt = op("vector", lambda e, ec=ec: e.tensor_tensor(
                out=gsc[:], in0=gsc[:], in1=sT[:, ec * 512:(ec + 1) * 512],
                op=ALU.mult), inc="sV")
            bank_set(ebanks[ecl], "sV", vt)
            last_s_guard[0] = ("sS", st)
            for tt in range(NT):
                bk, ptk, bn = transpose_f32(gsc[:, tt * 128:(tt + 1) * 128], ("sV", vt))
                wait("vector", "sP", ptk)
                wait("vector", "dXre", xt_last)
                col = tt * 512 + ecl * 128
                vtok = op("vector", lambda e, b3=bk, col=col: e.tensor_tensor(
                    out=outst[:, col:col + 128], in0=b3[:],
                    in1=xre[:, col:col + 128], op=ALU.add), inc="sV")
                bank_set(bn, "sV", vtok)
                G["gsc_free"] = ("sP", ptk)
        outv = cnt["sV"]
        wait("scalar", "sV", outv)
        lastout = None
        for tt in range(NT):
            r0 = base + tt * 128
            lastout = dma("scalar", "dA", dy[r0:r0 + 128, eh * 512:(eh + 1) * 512],
                          outst[:, tt * 512:(tt + 1) * 512])
        G["outst_free"] = ("dA", lastout)
        G["xre_free"] = ("sV", outv)

    for tt in range(NT):
        toks = ph1_front(0, tt)
        ph1_tr(0, tt, toks)
    G["arena_ready"] = (cnt["sS"], cnt["sV"])

    for stp in range(NSUP):
        nxt = stp + 1 if stp + 1 < NSUP else None
        ph2(stp)
        ph3(stp)
        xt0, mt0 = ph4_mm(stp, 0)
        if nxt is not None:
            for tt in (0, 1):
                toks = ph1_front(nxt, tt)
                ph1_tr(nxt, tt, toks)
        ph4_evac(stp, 0, xt0, mt0)
        xt1, mt1 = ph4_mm(stp, 1)
        if nxt is not None:
            for tt in (2, 3):
                toks = ph1_front(nxt, tt)
                ph1_tr(nxt, tt, toks)
        ph4_evac(stp, 1, xt1, mt1)
        if nxt is not None:
            G["arena_ready"] = (cnt["sS"], cnt["sV"])

    for s in semnames:
        if s.startswith("d"):
            wait("scalar", s, cnt[s])

    with nc.Block() as block:
        for eng in ENGS:
            ops = Q[eng]
            if not ops:
                continue
            getattr(block, eng)(lambda e, ops=ops: [f(e) for f in ops])

    ctx.close()
    return nc


_PROG = None


def kernel(L0_attn_x, L0_attn_W_el, L0_attn_b_el, L0_attn_W_le, L0_attn_b_le,
           L0_ffn_w_gate, L0_ffn_w_up, L0_ffn_w_down):
    global _PROG
    x = np.ascontiguousarray(L0_attn_x, dtype=np.float32)
    B, S, Emod = x.shape
    x2d = x.reshape(B * S, Emod)
    C = _host_consts()
    if _PROG is None:
        _PROG = build_program()
    nc = _PROG
    x0 = np.ascontiguousarray(x2d[0:1, :])
    ble = np.ascontiguousarray(np.asarray(L0_attn_b_le, dtype=np.float32)
                               .reshape(8, 128).T)
    common = {
        "x0": x0,
        "w_el": np.ascontiguousarray(L0_attn_W_el, dtype=np.float32),
        "b_el": np.ascontiguousarray(np.asarray(L0_attn_b_el, dtype=np.float32)
                                     .reshape(1, 24)),
        "w_le": np.ascontiguousarray(L0_attn_W_le, dtype=np.float32),
        "b_le": ble,
        "w_gate": np.ascontiguousarray(L0_ffn_w_gate, dtype=np.float32),
        "w_up": np.ascontiguousarray(L0_ffn_w_up, dtype=np.float32),
        "w_down": np.ascontiguousarray(L0_ffn_w_down, dtype=np.float32),
        "cos_hi": C["cos_hi"], "cos_lo": C["cos_lo"],
        "sin_hi": C["sin_hi"], "sin_lo": C["sin_lo"],
        "kraw": C["kraw"], "ident": C["ident"], "identb": C["identb"],
        "J": C["J"], "Ht": C["Ht"], "sc5": C["sc5"], "ones1": C["ones1"],
        "frD1": C["frD1"], "fiD1": C["fiD1"],
    }
    in_maps = []
    for cix in range(NCORES):
        m = dict(common)
        m["x"] = np.ascontiguousarray(x2d[cix * TOK:(cix + 1) * TOK, :])
        in_maps.append(m)
    res = run_bass_kernel_spmd(nc, in_maps, core_ids=list(range(NCORES)))
    out = np.concatenate([np.asarray(r["y"]).reshape(TOK, Emod)
                          for r in res.results], axis=0)
    return out.reshape(B, S, Emod)
